# version 17
# baseline (speedup 1.0000x reference)
import math
import numpy as np
from contextlib import ExitStack

import concourse.bass as bass
import concourse.mybir as mybir
from concourse.bass_utils import run_bass_kernel_spmd

F32 = mybir.dt.float32
BF16 = mybir.dt.bfloat16
AF = mybir.ActivationFunctionType
ALU = mybir.AluOpType
AX = mybir.AxisListType

ENGS = ("sync", "gpsimd", "tensor", "vector", "scalar")
W = 35840
D = 1024
T = 4096
NT = 32
NTO = 16
NP = 8

def _cp_def():
    c = {}
    def put(name, val):
        c[name] = (len(c), float(val))
    put("ONE", 1.0); put("HALF", 0.5); put("EPS", 1e-5)
    put("INV1024", 1.0 / 1024); put("NINV1024", -1.0 / 1024)
    hi = np.float32(0.2); put("C02HI", hi); put("C02LO", np.float32(0.2 - np.float64(hi)))
    put("NEG1", -1.0)
    put("ALPHA", 1.0 / (8.0 * math.sqrt(2.0))); put("BETA", 1.0 / math.sqrt(2.0))
    put("PI", math.pi); put("PIH", math.pi / 2)
    put("NR15", 1.5); put("NRN05", -0.5)
    hi = np.float32(1.0 / 220.0); put("I220HI", hi)
    put("I220LO", np.float32(1.0 / 220.0 - np.float64(hi)))
    for k in range(17):
        v = 2.0 * math.pi * (2.0 ** k)
        h = np.float32(v); l = np.float32(v - np.float64(h))
        put(f"KHI{k}", h); put(f"NKHI{k}", -h); put(f"NKLO{k}", -l)
    return c
_CP = _cp_def()
NCPOOL = 128


def _host_cpool():
    a = np.zeros((128, NCPOOL), np.float32)
    for name, (i, v) in _CP.items():
        a[:, i] = np.float32(v)
    return a


class Sched:
    NSL = {"sync": 16, "gpsimd": 6}

    def __init__(self):
        self.keys = ["tensor", "vector", "scalar", "gpsimdc"]
        for e_ in ("sync", "gpsimd"):
            self.keys += [f"{e_}{i}" for i in range(self.NSL[e_])]
        self.ops = {e_: [] for e_ in ENGS}
        self.tot = {k: 0 for k in self.keys}
        self.didx = {"sync": 0, "gpsimd": 0}
        self.pending = {e_: {} for e_ in ENGS}

    def add(self, eng, fn, waits=(), semkey=None):
        if semkey is None:
            semkey = eng
        wd = dict(self.pending[eng]); self.pending[eng] = {}
        if eng in ("vector", "scalar") and self.tot[eng] > 0:
            wd[eng] = self.tot[eng]
        for (tk, tv) in waits:
            if wd.get(tk, -1) < tv:
                wd[tk] = tv
        inc = 16 if (eng in ("sync", "gpsimd") and semkey != "gpsimdc") else 1
        self.ops[eng].append((fn, tuple(sorted(wd.items())), semkey, inc))
        self.tot[semkey] += inc
        return (semkey, self.tot[semkey])

    def dma(self, eng, out, in_, waits=()):
        i = self.didx[eng]; self.didx[eng] += 1
        key = f"{eng}{i % self.NSL[eng]}"
        w = list(waits)
        if self.tot[key]:
            w.append((key, self.tot[key]))
        return self.add(eng, lambda h, o=out, i_=in_: h.dma_start(out=o, in_=i_),
                        w, semkey=key)

    def snapshot(self):
        return [(k, v_) for k, v_ in self.tot.items() if v_ > 0]

    def barrier(self):
        snap = self.snapshot()
        for e_ in ENGS:
            for (k, v_) in snap:
                if self.pending[e_].get(k, -1) < v_:
                    self.pending[e_][k] = v_
        return snap


def ap(t, off, dims):
    return bass.AP(t, off, dims)


def build_program(debug=False):
    nc = bass.Bass("TRN2", target_bir_lowering=False)

    x_d = nc.dram_tensor("x", [T, D], F32, kind="ExternalInput")
    wq_d = nc.dram_tensor("wq", [D, D], F32, kind="ExternalInput")
    wkv_d = nc.dram_tensor("wkv", [D, 2 * D], F32, kind="ExternalInput")
    wo_d = nc.dram_tensor("wo", [D, D], F32, kind="ExternalInput")
    lnw_d = nc.dram_tensor("lnw", [1, D], F32, kind="ExternalInput")
    lnb_d = nc.dram_tensor("lnb", [1, D], F32, kind="ExternalInput")
    fb_d = nc.dram_tensor("fbias", [1, D], F32, kind="ExternalInput")
    wfr_d = nc.dram_tensor("wfr", [D, D], F32, kind="ExternalInput")
    wfi_d = nc.dram_tensor("wfi", [D, D], F32, kind="ExternalInput")
    ct_d = nc.dram_tensor("ctab", [D, D], F32, kind="ExternalInput")
    st_d = nc.dram_tensor("stab", [D, D], F32, kind="ExternalInput")
    th_d = nc.dram_tensor("theta", [128, 1], F32, kind="ExternalInput")
    cp_d = nc.dram_tensor("cpool", [128, NCPOOL], F32, kind="ExternalInput")
    pv_d = nc.dram_tensor("pvals", [128, 1], F32, kind="ExternalInput")
    tv_d = nc.dram_tensor("t128vals", [32, 1], F32, kind="ExternalInput")
    lhi_d = nc.dram_tensor("lhi", [128, 32], F32, kind="ExternalInput")
    llo_d = nc.dram_tensor("llo", [128, 32], F32, kind="ExternalInput")
    id_d = nc.dram_tensor("ident", [128, 128], F32, kind="ExternalInput")
    out_d = nc.dram_tensor("out", [2048, D], F32, kind="ExternalOutput")
    skind = "ExternalOutput" if debug else "Internal"
    ht3_d = nc.dram_tensor("ht3", [NT, D, 128], F32, kind=skind)
    xt3_d = nc.dram_tensor("xt3", [NTO, D, 128], F32, kind=skind)
    wv3_d = nc.dram_tensor("wv3", [NP, 128, 2048], F32, kind=skind)
    if debug:
        dbg_ecos = nc.dram_tensor("dbg_ecos", [128, 1024], F32, kind="ExternalOutput")
        dbg_esin = nc.dram_tensor("dbg_esin", [128, 2048], F32, kind="ExternalOutput")
        dbg_m = nc.dram_tensor("dbg_m", [128, 8192], F32, kind="ExternalOutput")
        dbg_kp = nc.dram_tensor("dbg_kp", [128, 4096], F32, kind="ExternalOutput")
        dbg_qp = nc.dram_tensor("dbg_qp", [128, 2048], F32, kind="ExternalOutput")
        dbg_vp = nc.dram_tensor("dbg_vp", [128, 4224], F32, kind="ExternalOutput")
        dbg_sm = nc.dram_tensor("dbg_sm", [128, 512], F32, kind="ExternalOutput")
        dbg_sm2 = nc.dram_tensor("dbg_sm2", [32, 256], F32, kind="ExternalOutput")
        dbg_xn = nc.dram_tensor("dbg_xn", [128, 1024], F32, kind="ExternalOutput")
        dbg_hh = nc.dram_tensor("dbg_hh", [128, 1024], F32, kind="ExternalOutput")
        dbg_st = nc.dram_tensor("dbg_st", [128, 48], F32, kind="ExternalOutput")

    s = Sched()
    with ExitStack() as ctx:
        e = ctx.enter_context
        A = e(nc.sbuf_tensor("A", [128, W], F32))
        CPO = e(nc.sbuf_tensor("CPO", [128, NCPOOL], F32))
        PV = e(nc.sbuf_tensor("PV", [128, 1], F32))
        TV = e(nc.sbuf_tensor("TV", [32, 1], F32))
        LHI = e(nc.sbuf_tensor("LHI", [128, 32], F32))
        LLO = e(nc.sbuf_tensor("LLO", [128, 32], F32))
        IDN = e(nc.sbuf_tensor("IDN", [128, 128], F32))
        TH = e(nc.sbuf_tensor("TH", [128, 1], F32))
        ONESM = e(nc.sbuf_tensor("ONESM", [128, 128], F32))
        LNWR = e(nc.sbuf_tensor("LNWR", [128, D], F32))
        LNBR = e(nc.sbuf_tensor("LNBR", [128, D], F32))
        BIAR = e(nc.sbuf_tensor("BIAR", [128, D], F32))
        ECOS = e(nc.sbuf_tensor("ECOS", [128, 1024], F32))
        ESIN = e(nc.sbuf_tensor("ESIN", [128, 2048], F32))
        MBUF = e(nc.sbuf_tensor("MBUF", [128, 8192], F32))
        SM = e(nc.sbuf_tensor("SM", [128, 512], F32))
        SM2 = e(nc.sbuf_tensor("SM2", [32, 256], F32))
        WROW = A[0:1, 2048:5120]
        C1024 = A[0:1, 5120:6144]
        S1024 = A[0:1, 6144:7168]
        P2 = [e(nc.psum_tensor(f"P2_{i}", [128, 1024], F32)) for i in range(4)]
        sems = {k: e(nc.semaphore(f"sem_{k}")) for k in s.keys}
        blk = e(nc.Block())

        def bank(i):
            return P2[i // 2][:, (i % 2) * 512:(i % 2) * 512 + 512]

        def cpc(name):
            i = _CP[name][0]
            return CPO[:, i:i + 1]

        def cpc_n(name, n):
            i = _CP[name][0]
            return CPO[0:n, i:i + 1]

        def v(fn, waits=()):
            return s.add("vector", fn, waits)

        def sc(fn, waits=()):
            return s.add("scalar", fn, waits)

        def tn(fn, waits=()):
            return s.add("tensor", fn, waits)

        def reduce_chain(g, scr, kmax, npart):
            for k in range(kmax, -1, -1):
                v(lambda h, g=g, scr=scr, k=k, n=npart: h.tensor_scalar(
                    scr, g, cpc_n(f"KHI{k}", n), None, ALU.is_ge))
                v(lambda h, g=g, scr=scr, k=k, n=npart: h.scalar_tensor_tensor(
                    g, scr, cpc_n(f"NKHI{k}", n), g, ALU.mult, ALU.add))
                v(lambda h, g=g, scr=scr, k=k, n=npart: h.scalar_tensor_tensor(
                    g, scr, cpc_n(f"NKLO{k}", n), g, ALU.mult, ALU.add))

        def wrap_pm_pi(g, scr, npart):
            v(lambda h, g=g, scr=scr, n=npart: h.tensor_scalar(
                scr, g, cpc_n("PI", n), None, ALU.is_ge))
            v(lambda h, g=g, scr=scr, n=npart: h.scalar_tensor_tensor(
                g, scr, cpc_n("NKHI0", n), g, ALU.mult, ALU.add))
            v(lambda h, g=g, scr=scr, n=npart: h.scalar_tensor_tensor(
                g, scr, cpc_n("NKLO0", n), g, ALU.mult, ALU.add))

        s.dma("sync", CPO[:, :], cp_d[:, :])
        d_pv = s.dma("sync", PV[:, :], pv_d[:, :])
        d_tv = s.dma("sync", TV[:, :], tv_d[:, :])
        s.dma("sync", LHI[:, :], lhi_d[:, :])
        d_llo = s.dma("sync", LLO[:, :], llo_d[:, :])
        s.dma("sync", IDN[:, :], id_d[:, :])
        d_th = s.dma("sync", TH[:, :], th_d[:, :])
        s.dma("sync", WROW[:, 0:D], lnw_d[:, :])
        s.dma("sync", WROW[:, D:2 * D], lnb_d[:, :])
        d_w3 = s.dma("sync", WROW[:, 2 * D:3 * D], fb_d[:, :])
        d_ph0 = s.snapshot()

        m_ones = v(lambda h: h.memset(ONESM[:, :], 1.0))
        v(lambda h: h.memset(SM[:, :], 0.0))
        v(lambda h: h.memset(SM2[:, :], 0.0))

        reps = []
        for i, (so, dst) in enumerate(((0, LNWR), (D, LNBR), (2 * D, BIAR))):
            pt = P2[2 + (i % 2)]
            w_extra = [reps[i - 2]] if i >= 2 else []
            for half in range(2):
                tn(lambda h, pt=pt, half=half, so=so: h.matmul(
                    pt[:, half * 512:half * 512 + 512], ONESM[0:1, :],
                    WROW[:, so + half * 512: so + half * 512 + 512],
                    start=True, stop=True),
                   waits=d_ph0 + [m_ones] + w_extra)
            mm_done = ("tensor", s.tot["tensor"])
            reps.append(v(lambda h, pt=pt, dst=dst: h.tensor_copy(dst[:, :], pt[:, :]),
                          waits=[mm_done]))

        Lsum = SM[:, 0:32]; evec = SM[:, 32:64]; fbv = SM[:, 64:96]
        th220 = SM[:, 96:97]; fbv2 = SM[:, 160:192]
        FRR = SM[:, 192:224]
        m = v(lambda h: h.tensor_tensor(Lsum, LHI[:, :], LLO[:, :], ALU.add),
              waits=d_ph0)
        m = sc(lambda h: h.activation(evec, Lsum, AF.Exp), waits=[m] + d_ph0)
        v(lambda h: h.tensor_scalar(fbv, evec, cpc("ONE"), None, ALU.subtract),
          waits=[m])
        v(lambda h: h.tensor_scalar(fbv2, fbv, cpc("C02HI"), None, ALU.mult))
        v(lambda h: h.scalar_tensor_tensor(fbv2, fbv, cpc("C02LO"), fbv2,
                                           ALU.mult, ALU.add))
        v(lambda h: h.tensor_scalar(th220, TH[:, :], cpc("I220HI"), None, ALU.mult))
        v(lambda h: h.scalar_tensor_tensor(th220, TH[:, :], cpc("I220LO"), th220,
                                           ALU.mult, ALU.add))
        m_frr = v(lambda h: h.tensor_scalar(FRR, fbv2, ap(SM, 96, [[512, 128], [1, 1]]),
                                            None, ALU.mult))

        ANGP = SM[:, 224:256]; SCR = SM[:, 256:288]; GC = SM[:, 288:320]
        CP_ = SM[:, 320:352]; SP_ = SM[:, 352:384]
        v(lambda h: h.tensor_scalar(ANGP, FRR, PV[:, :], None, ALU.mult))
        reduce_chain(ANGP, SCR, 11, 128)
        v(lambda h: h.tensor_scalar(GC, ANGP, cpc("PIH"), None, ALU.add))
        wrap_pm_pi(GC, SCR, 128)
        m = ("vector", s.tot["vector"])
        m_cp = sc(lambda h: h.activation(CP_, GC, AF.Sin), waits=[m])
        wrap_pm_pi(ANGP, SCR, 128)
        m = ("vector", s.tot["vector"])
        m_ps = sc(lambda h: h.activation(SP_, ANGP, AF.Sin), waits=[m])

        ANGT = SM2[:, 0:32]; SCR2 = SM2[:, 32:64]; GC2 = SM2[:, 64:96]
        CT_ = SM2[:, 96:128]; ST_ = SM2[:, 128:160]
        v(lambda h: h.tensor_scalar(ANGT, FRR[0:32, :], TV[:, :], None, ALU.mult),
          waits=[m_frr])
        reduce_chain(ANGT, SCR2, 16, 32)
        v(lambda h: h.tensor_scalar(GC2, ANGT, cpc_n("PIH", 32), None, ALU.add))
        wrap_pm_pi(GC2, SCR2, 32)
        m = ("vector", s.tot["vector"])
        m_ct = sc(lambda h: h.activation(CT_, GC2, AF.Sin), waits=[m])
        wrap_pm_pi(ANGT, SCR2, 32)
        m = ("vector", s.tot["vector"])
        m_ts = sc(lambda h: h.activation(ST_, ANGT, AF.Sin), waits=[m])

        g1 = s.dma("gpsimd", C1024, CT_, waits=[m_ct, m_ts])
        g2 = s.dma("gpsimd", S1024, ST_, waits=[m_ct, m_ts])
        for half in range(2):
            tn(lambda h, half=half: h.matmul(
                P2[0][:, half * 512:half * 512 + 512], ONESM[0:1, :],
                C1024[:, half * 512:half * 512 + 512], start=True, stop=True),
               waits=[g1])
            tn(lambda h, half=half: h.matmul(
                P2[1][:, half * 512:half * 512 + 512], ONESM[0:1, :],
                S1024[:, half * 512:half * 512 + 512], start=True, stop=True),
               waits=[g2])
        m_rep = ("tensor", s.tot["tensor"])

        CPV = ap(SM, 320, [[512, 128], [0, 32], [1, 32]])
        SPV = ap(SM, 352, [[512, 128], [0, 32], [1, 32]])
        TA = A[:, 0:1024]
        TAv = ap(A, 0, [[W, 128], [32, 32], [1, 32]])
        TBv = ap(A, 1024, [[W, 128], [32, 32], [1, 32]])
        P0v = ap(P2[0], 0, [[1024, 128], [32, 32], [1, 32]])
        P1v = ap(P2[1], 0, [[1024, 128], [32, 32], [1, 32]])
        ECv = ap(ECOS, 0, [[1024, 128], [32, 32], [1, 32]])

        v(lambda h: h.tensor_tensor(TAv, P0v, CPV, ALU.mult),
          waits=[m_rep, m_cp, m_ps])
        v(lambda h: h.tensor_tensor(TBv, P1v, SPV, ALU.mult))
        v(lambda h: h.tensor_tensor(ECv, TAv, TBv, ALU.subtract))
        v(lambda h: h.tensor_tensor(TAv, P1v, CPV, ALU.mult))
        v(lambda h: h.tensor_tensor(TBv, P0v, SPV, ALU.mult))
        v(lambda h: h.tensor_tensor(TAv, TAv, TBv, ALU.add))
        v(lambda h: h.tensor_scalar(ap(ESIN, 0, [[2048, 128], [2, 1024]]),
                                    TA[:, :], cpc("NEG1"), None, ALU.mult))
        mes = v(lambda h: h.tensor_scalar(ap(ESIN, 1, [[2048, 128], [2, 1024]]),
                                          TA[:, :], cpc("ONE"), None, ALU.mult))
        if debug:
            s.dma("gpsimd", dbg_ecos[:, :], ECOS[:, :], waits=[mes])
            s.dma("gpsimd", dbg_esin[:, :], ESIN[:, :], waits=[mes])
            s.dma("gpsimd", dbg_sm[:, :], SM[:, :], waits=[mes])
            s.dma("gpsimd", dbg_sm2[:, :], SM2[:, :], waits=[mes])

        s.barrier()

        CB, SB2 = A[:, 0:4096], A[:, 4096:8192]
        WRS, WIS, WIN = A[:, 8192:9216], A[:, 9216:10240], A[:, 10240:11264]
        PRT_off, PIT_off = 11264, 19456
        WR2 = [WRS, MBUF[:, 0:1024], MBUF[:, 1024:2048], MBUF[:, 2048:3072]]
        WI2 = [WIS, MBUF[:, 3072:4096], MBUF[:, 4096:5120], MBUF[:, 5120:6144]]
        WN2 = [WIN, MBUF[:, 6144:7168]]

        def cs_block(col0, waits):
            dt_ = col0 // 512
            t1 = s.dma("sync", CB[:, :],
                       ap(ct_d, dt_ * 524288, [[4096, 128], [1, 4096]]), waits=waits)
            t2 = s.dma("gpsimd", SB2[:, :],
                       ap(st_d, dt_ * 524288, [[4096, 128], [1, 4096]]), waits=waits)
            return [t1, t2]

        p1s = {"ev_prev": None, "dcs": None, "mprev": None, "sc_ev": {},
               "pend_evac": None, "mm_hist": {}}

        def ph1_mms(i):
            if i >= 32:
                return
            if i < 16:
                dt, oc = i // 8, i % 8
                sl_ = i % 4
                sn_ = i % 2
                if oc == 0:
                    p1s["mprev"] = ("tensor", s.tot["tensor"])
                    p1s["dcs"] = cs_block(dt * 512, [p1s["mprev"]])
                mprev, dcs = p1s["mprev"], p1s["dcs"]
                wpre = [p1s["mm_hist"][i - 4]] if i >= 4 else []
                dwr = s.dma("sync", WR2[sl_][:, :],
                            ap(wfr_d, oc * 131072, [[1024, 128], [1, 1024]]),
                            waits=wpre)
                dwi = s.dma("gpsimd", WI2[sl_][:, :],
                            ap(wfi_d, oc * 131072, [[1024, 128], [1, 1024]]),
                            waits=wpre)
                mneg = sc(lambda h, sl_=sl_, sn_=sn_: h.activation(
                    WN2[sn_][:, :], WI2[sl_][:, :], AF.Identity, scale=cpc("NEG1")),
                          waits=[dwi] + ([p1s["mm_hist"][i - 2]] if i >= 2 else []))
                bpr, bpi = bank(0 + (oc % 2)), bank(2 + (oc % 2))
                w0 = [dwr, dwi] + dcs + ([p1s["ev_prev"]] if p1s["ev_prev"] else [])
                for fc in range(8):
                    tn(lambda h, b=bpr, fc=fc, sl_=sl_: h.matmul(
                        b, WR2[sl_][:, fc * 128:fc * 128 + 128],
                        CB[:, fc * 512:fc * 512 + 512], start=(fc == 0), stop=False),
                       waits=w0 if fc == 0 else [])
                    tn(lambda h, b=bpr, fc=fc, sl_=sl_: h.matmul(
                        b, WI2[sl_][:, fc * 128:fc * 128 + 128],
                        SB2[:, fc * 512:fc * 512 + 512], start=False, stop=(fc == 7)))
                for fc in range(8):
                    tn(lambda h, b=bpi, fc=fc, sl_=sl_: h.matmul(
                        b, WR2[sl_][:, fc * 128:fc * 128 + 128],
                        SB2[:, fc * 512:fc * 512 + 512], start=(fc == 0), stop=False),
                       waits=[mneg] if fc == 0 else [])
                    tn(lambda h, b=bpi, fc=fc, sn_=sn_: h.matmul(
                        b, WN2[sn_][:, fc * 128:fc * 128 + 128],
                        CB[:, fc * 512:fc * 512 + 512], start=False, stop=(fc == 7)))
                p1s["mprev"] = ("tensor", s.tot["tensor"])
                p1s["mm_hist"][i] = p1s["mprev"]
                pr_dst = A[:, PRT_off + oc * 1024 + dt * 512:
                           PRT_off + oc * 1024 + dt * 512 + 512]
                pi_dst = A[:, PIT_off + oc * 1024 + dt * 512:
                           PIT_off + oc * 1024 + dt * 512 + 512]
                p1s["pend_evac"] = (bpr, pr_dst, bpi, pi_dst, p1s["mprev"])
            else:
                j = i - 16
                mt, dc = j // 8, j % 8
                if dc == 0:
                    p1s["mprev"] = ("tensor", s.tot["tensor"])
                    p1s["dcs"] = cs_block(mt * 512, [p1s["mprev"]])
                b = bank(4 + (dc % 2))
                li = mt * 8 + dc
                wts = list(p1s["dcs"]) + [p1s["ev_prev"]]
                if li >= 2:
                    wts.append(p1s["sc_ev"][li - 2])
                for oc in range(8):
                    tn(lambda h, b=b, oc=oc, dc=dc: h.matmul(
                        b, A[:, PRT_off + oc * 1024 + dc * 128:
                             PRT_off + oc * 1024 + dc * 128 + 128],
                        CB[:, oc * 512:oc * 512 + 512], start=(oc == 0), stop=False),
                       waits=wts if oc == 0 else [])
                    tn(lambda h, b=b, oc=oc, dc=dc: h.matmul(
                        b, A[:, PIT_off + oc * 1024 + dc * 128:
                             PIT_off + oc * 1024 + dc * 128 + 128],
                        SB2[:, oc * 512:oc * 512 + 512], start=False, stop=(oc == 7)))
                mdone = ("tensor", s.tot["tensor"])
                p1s["sc_ev"][li] = sc(
                    lambda h, b=b, dc=dc, mt=mt: h.activation(
                        MBUF[:, dc * 1024 + mt * 512: dc * 1024 + mt * 512 + 512], b,
                        AF.Identity, scale=cpc("INV1024")),
                    waits=[mdone])

        def ph1_evacs(i):
            if p1s["pend_evac"] is None:
                return
            bpr, pr_dst, bpi, pi_dst, mprev = p1s["pend_evac"]
            p1s["pend_evac"] = None
            sc(lambda h, b=bpr, dst=pr_dst: h.activation(dst, b, AF.Identity),
               waits=[mprev])
            p1s["ev_prev"] = sc(lambda h, b=bpi, dst=pi_dst: h.activation(
                dst, b, AF.Identity))

        XB = [A[:, 27648:28672], A[:, 28672:29696]]
        D1_ = [A[:, 29696:30720], A[:, 30720:31744]]
        D2_ = [A[:, 31744:32768], A[:, 32768:33792]]
        XN_ = [A[:, 33792:34816], A[:, 34816:35840]]
        SQS = XB
        HH_ = XB
        XT = D1_
        HT = D2_

        def stat(par, i):
            o = 384 + par * 24 + i
            return SM[:, o:o + 1]

        def ln_stats(par, i0, src, dst, w0):
            m2 = v(lambda h, p=par, i0=i0, src=src: (
                h.reduce_sum(stat(p, i0), src, axis=AX.X)), waits=w0)
            m2 = v(lambda h, p=par, i0=i0: h.tensor_scalar(
                stat(p, i0 + 1), stat(p, i0), cpc("NINV1024"), None, ALU.mult))
            sc(lambda h, p=par, i0=i0, src=src, dst=dst: h.activation(
                dst, src, AF.Identity, bias=stat(p, i0 + 1)), waits=[m2] + w0)
            m3 = sc(lambda h, p=par, i0=i0, dst=dst: h.activation(
                SQS[p], dst, AF.Square, accum_out=stat(p, i0 + 2)))
            m4 = v(lambda h, p=par, i0=i0: h.tensor_scalar(
                stat(p, i0 + 3), stat(p, i0 + 2), cpc("INV1024"), None, ALU.mult),
                waits=[m3])
            m5 = sc(lambda h, p=par, i0=i0: h.activation(
                stat(p, i0 + 4), stat(p, i0 + 3), AF.Sqrt, bias=cpc("EPS")),
                waits=[m4])
            v(lambda h, p=par, i0=i0: h.reciprocal(stat(p, i0 + 5), stat(p, i0 + 4)),
              waits=[m5])
            v(lambda h, p=par, i0=i0: h.tensor_scalar(
                stat(p, i0 + 6), stat(p, i0 + 3), cpc("EPS"), None, ALU.add))
            for _ in range(2):
                v(lambda h, p=par, i0=i0: h.tensor_tensor(
                    stat(p, i0 + 7), stat(p, i0 + 5), stat(p, i0 + 5), ALU.mult))
                v(lambda h, p=par, i0=i0: h.tensor_tensor(
                    stat(p, i0 + 7), stat(p, i0 + 7), stat(p, i0 + 6), ALU.mult))
                v(lambda h, p=par, i0=i0: h.tensor_scalar(
                    stat(p, i0 + 7), stat(p, i0 + 7), cpc("NRN05"), cpc("NR15"),
                    ALU.mult, ALU.add))
                v(lambda h, p=par, i0=i0: h.tensor_tensor(
                    stat(p, i0 + 5), stat(p, i0 + 5), stat(p, i0 + 7), ALU.mult))
            return ("vector", s.tot["vector"])

        tile_end = {}
        ph2m = {}

        def ph2_ln(tt):
            par = tt % 2
            wprev = tile_end.get(tt - 2, [])
            dx = s.dma("sync", XB[par], x_d[tt * 128:(tt + 1) * 128, :], waits=wprev)
            ln_stats(par, 0, XB[par], D1_[par], [dx] + wprev)
            m6 = v(lambda h, p=par: h.tensor_scalar(XN_[p], D1_[p], stat(p, 5),
                                                    None, ALU.mult))
            ln_stats(par, 8, XN_[par], D2_[par], [m6])
            v(lambda h, p=par: h.tensor_scalar(HH_[p], D2_[p], stat(p, 13),
                                               None, ALU.mult))
            v(lambda h, p=par: h.tensor_tensor(HH_[p], HH_[p], LNWR[:, :], ALU.mult))
            mh = v(lambda h, p=par: h.tensor_tensor(HH_[p], HH_[p], LNBR[:, :],
                                                    ALU.add))
            ph2m[tt] = (mh, m6)

        last_tr_ev = [None]

        def ph2_tr(tt):
            par = tt % 2
            own = tt < NTO
            mh, m6 = ph2m[tt]
            pb = P2[3]
            w_pb = [last_tr_ev[0]] if last_tr_ev[0] else []
            for dc in range(8):
                tn(lambda h, p=par, dc=dc: h.transpose(
                    pb[:, dc * 128:dc * 128 + 128],
                    HH_[p][:, dc * 128:dc * 128 + 128], IDN[:, :]),
                   waits=([mh] + w_pb) if dc == 0 else [])
            mt1 = ("tensor", s.tot["tensor"])
            ev1 = v(lambda h, p=par: h.tensor_copy(HT[p], pb[:, :]), waits=[mt1])
            s.dma("gpsimd",
                  ap(ht3_d, tt * D * 128, [[128, 128], [16384, 8], [1, 128]]),
                  HT[par], waits=[ev1])
            last_tr_ev[0] = ev1
            if own:
                for dc in range(8):
                    tn(lambda h, p=par, dc=dc: h.transpose(
                        pb[:, dc * 128:dc * 128 + 128],
                        XN_[p][:, dc * 128:dc * 128 + 128], IDN[:, :]),
                       waits=[m6, ev1] if dc == 0 else [])
                mt2 = ("tensor", s.tot["tensor"])
                ev2 = v(lambda h, p=par: h.tensor_copy(XT[p], pb[:, :]), waits=[mt2])
                s.dma("gpsimd",
                      ap(xt3_d, tt * D * 128, [[128, 128], [16384, 8], [1, 128]]),
                      XT[par], waits=[ev2])
                last_tr_ev[0] = ev2
            tile_end[tt] = s.snapshot()

        for i in range(NT + 2):
            ph1_mms(i)
            if 2 <= i:
                ph2_tr(i - 2)
            if i < NT:
                ph2_ln(i)
            ph1_evacs(i)

        if debug:
            s.dma("gpsimd", dbg_m[:, :], MBUF[:, :],
                  waits=[("scalar", s.tot["scalar"])])
            mend = ("vector", s.tot["vector"])
            send = ("scalar", s.tot["scalar"])
            s.dma("gpsimd", dbg_xn[:, :], XN_[1], waits=[mend, send])
            s.dma("gpsimd", dbg_hh[:, :], HH_[1], waits=[mend])
            s.dma("gpsimd", dbg_st[:, :], SM[:, 384:432], waits=[mend])

        s.barrier()

        WSLAB_off = 0
        HTL = [A[:, 3072:4096], A[:, 4096:5120],
               A[:, 30000:31024], A[:, 31024:32048]]
        KPH = A[:, 5120:7168].bitcast(BF16)
        KPL = A[:, 7168:9216].bitcast(BF16)
        QPH = A[:, 9216:10240].bitcast(BF16)
        QPL = A[:, 10240:11264].bitcast(BF16)
        VPH = A[:, 11264:13376].bitcast(BF16)
        VPL = A[:, 13376:15488].bitcast(BF16)
        U32 = [A[:, 15488 + i * 512: 15488 + i * 512 + 512] for i in range(8)]
        T1_, T2_ = A[:, 19584:19712], A[:, 19712:19840]
        KR_ = [A[:, 19840:19968], A[:, 29600:29728]]
        QR_ = [A[:, 19968:20096], A[:, 29728:29856]]
        WV_off = 20096
        CSH_off = 24704
        AVS2 = [[A[:, 30000 + (2 * qp + hh) * 512: 30000 + (2 * qp + hh) * 512 + 512]
                 for hh in range(2)] for qp in range(2)]
        RDEN2 = [[A[64:65, 32048 + (2 * qp + hh) * 512: 32048 + (2 * qp + hh) * 512 + 512]
                  for hh in range(2)] for qp in range(2)]
        UH = [A[:, 25248 + i * 256: 25248 + i * 256 + 256].bitcast(BF16) for i in range(8)]
        UL = [A[:, 27296 + i * 256: 27296 + i * 256 + 256].bitcast(BF16) for i in range(8)]
        HALFB = A[:, 29900:29901].bitcast(BF16)

        wv_spill = {}
        for p_ in range(NP):
            dk = s.dma("sync", ap(A, WSLAB_off, [[W, 128], [384, 8], [1, 128]]),
                       ap(wkv_d, 128 * p_, [[2048, 128], [262144, 8], [1, 128]]))
            dv2 = s.dma("sync", ap(A, WSLAB_off + 128, [[W, 128], [384, 8], [1, 128]]),
                        ap(wkv_d, D + 128 * p_, [[2048, 128], [262144, 8], [1, 128]]))
            dq = s.dma("sync", ap(A, WSLAB_off + 256, [[W, 128], [384, 8], [1, 128]]),
                       ap(wq_d, 128 * p_, [[1024, 128], [131072, 8], [1, 128]]))
            dwt = [dk, dv2, dq]
            mhb = v(lambda h: h.memset(HALFB[:, 0:1], 0.5))
            v(lambda h: h.memset(VPH[:, :], 1.0))
            mvp1 = v(lambda h: h.memset(VPL[:, :], 0.0))
            ev_tile, ev_tile_t = {}, {}
            rope_done, rope_doneq, vcp_done, ps_free = {}, {}, {}, {}

            def emit_tr(tt):
                par = tt % 2
                own = tt < NTO
                tb = par * 512
                wv_w = [ev_tile[tt - 2]] if tt >= 2 else []
                tn(lambda h, tb=tb, par=par: h.transpose(
                    P2[3][:, tb:tb + 128], KR_[par][:, :], IDN[:, :]),
                   waits=[rope_done[tt]] + wv_w)
                mtr = ("tensor", s.tot["tensor"])
                v(lambda h, tt=tt, tb=tb: h.tensor_copy(
                    KPH[:, tt * 128:tt * 128 + 128], P2[3][:, tb:tb + 128]),
                  waits=[mtr])
                ev_k = v(lambda h, tt=tt, tb=tb: h.tensor_tensor(
                    KPL[:, tt * 128:tt * 128 + 128], P2[3][:, tb:tb + 128],
                    KPH[:, tt * 128:tt * 128 + 128], ALU.subtract))
                mtr_last = mtr
                if own:
                    tn(lambda h, tb=tb, par=par: h.transpose(
                        P2[3][:, tb + 128:tb + 256], QR_[par][:, :], IDN[:, :]),
                       waits=[rope_doneq[tt]])
                    mtr2 = ("tensor", s.tot["tensor"])
                    v(lambda h, tt=tt, tb=tb: h.tensor_copy(
                        QPH[:, tt * 128:tt * 128 + 128], P2[3][:, tb + 128:tb + 256]),
                      waits=[mtr2])
                    ev_k = v(lambda h, tt=tt, tb=tb: h.tensor_tensor(
                        QPL[:, tt * 128:tt * 128 + 128], P2[3][:, tb + 128:tb + 256],
                        QPH[:, tt * 128:tt * 128 + 128], ALU.subtract))
                    mtr_last = mtr2
                for hh2 in range(2):
                    for pl in (VPH, VPL):
                        tn(lambda h, kc=tt, hh=hh2, pl=pl: h.matmul(
                            bank(4 + hh)[0:66, 0:1],
                            pl[:, kc * 132 + 66 * hh: kc * 132 + 66 * hh + 66],
                            HALFB[:, 0:1],
                            start=(kc == 0 and pl is VPH),
                            stop=(kc == NT - 1 and pl is VPL)),
                           waits=[vcp_done[tt], mhb] if (hh2 == 0 and pl is VPH) else [])
                ev_tile[tt] = max(ev_k, vcp_done[tt])
                ev_tile_t[tt] = mtr_last

            mproj_hist = {}
            for tt in range(NT):
                par = tt % 2
                hb = tt % 4
                own = tt < NTO
                wv_w = [ps_free[tt - 2]] if tt >= 2 else []
                wt_w = [mproj_hist[tt - 4]] if tt >= 4 else []
                dh = s.dma("sync", HTL[hb][:, 0:512],
                           ap(ht3_d, tt * D * 128, [[128, 128], [16384, 4], [1, 128]]),
                           waits=wt_w)
                dh2 = s.dma("gpsimd", HTL[hb][:, 512:1024],
                            ap(ht3_d, tt * D * 128 + 4 * 16384,
                               [[128, 128], [16384, 4], [1, 128]]),
                            waits=wt_w)
                ncols = 384 if own else 256
                pb = bank(par)
                for dc in range(8):
                    tn(lambda h, pb=pb, dc=dc, nc_=ncols, hb=hb: h.matmul(
                        pb[:, 0:nc_], HTL[hb][:, dc * 128:dc * 128 + 128],
                        A[:, WSLAB_off + dc * 384: WSLAB_off + dc * 384 + nc_],
                        start=(dc == 0), stop=(dc == 7)),
                       waits=([dh] + dwt + wv_w) if dc == 0 else
                       ([dh2] if dc == 4 else []))
                mproj = ("tensor", s.tot["tensor"])
                mproj_hist[tt] = mproj
                off0 = par * 512
                ps_t = P2[0]
                kcos = ap(ps_t, off0, [[1024, 128], [64, 2], [2, 32], [1, 2]])
                kswp = ap(ps_t, off0 + 1, [[1024, 128], [64, 2], [2, 32], [-1, 2]])
                ecv = ap(ECOS, tt * 32, [[1024, 128], [0, 2], [1, 32], [0, 2]])
                esv = ap(ESIN, tt * 64, [[2048, 128], [0, 2], [2, 32], [1, 2]])
                t1v = ap(A, 19584, [[W, 128], [64, 2], [2, 32], [1, 2]])
                t2v = ap(A, 19712, [[W, 128], [64, 2], [2, 32], [1, 2]])
                v(lambda h, kcos=kcos, ecv=ecv, t1v=t1v: h.tensor_tensor(
                    t1v, kcos, ecv, ALU.mult), waits=[mproj])
                v(lambda h, kswp=kswp, esv=esv, t2v=t2v: h.tensor_tensor(
                    t2v, kswp, esv, ALU.mult))
                rope_done[tt] = v(lambda h, par=par: h.tensor_tensor(
                    KR_[par][:, :], T1_[:, :], T2_[:, :], ALU.add),
                    waits=[ev_tile_t[tt - 2]] if tt >= 2 else [])
                for hh2 in range(2):
                    v(lambda h, off0=off0, tt=tt, hh2=hh2: h.tensor_copy(
                        VPH[:, tt * 132 + 66 * hh2: tt * 132 + 66 * hh2 + 64],
                        ps_t[:, off0 + 128 + 64 * hh2: off0 + 192 + 64 * hh2]))
                    mvcp = v(lambda h, off0=off0, tt=tt, hh2=hh2: h.tensor_tensor(
                        VPL[:, tt * 132 + 66 * hh2: tt * 132 + 66 * hh2 + 64],
                        ps_t[:, off0 + 128 + 64 * hh2: off0 + 192 + 64 * hh2],
                        VPH[:, tt * 132 + 66 * hh2: tt * 132 + 66 * hh2 + 64],
                        ALU.subtract))
                vcp_done[tt] = mvcp
                if own:
                    qcos = ap(ps_t, off0 + 256, [[1024, 128], [64, 2], [2, 32], [1, 2]])
                    qswp = ap(ps_t, off0 + 257, [[1024, 128], [64, 2], [2, 32], [-1, 2]])
                    v(lambda h, qcos=qcos, ecv=ecv, t1v=t1v: h.tensor_tensor(
                        t1v, qcos, ecv, ALU.mult))
                    v(lambda h, qswp=qswp, esv=esv, t2v=t2v: h.tensor_tensor(
                        t2v, qswp, esv, ALU.mult))
                    rope_doneq[tt] = v(lambda h, par=par: h.tensor_tensor(
                        QR_[par][:, :], T1_[:, :], T2_[:, :], ALU.add))
                ps_free[tt] = rope_doneq[tt] if own else vcp_done[tt]
                if tt >= 1:
                    emit_tr(tt - 1)
            emit_tr(NT - 1)
            m_projv = ("vector", s.tot["vector"])
            if debug and p_ == 0:
                s.dma("gpsimd", dbg_kp[:, :], A[:, 5120:9216], waits=[m_projv])
                s.dma("gpsimd", dbg_qp[:, :], A[:, 9216:11264], waits=[m_projv])
                s.dma("gpsimd", dbg_vp[:, :], A[:, 11264:15488], waits=[m_projv])
            mcs = ("tensor", s.tot["tensor"])
            cs_ev = []
            for hh in range(2):
                cs_ev.append(v(lambda h, hh=hh: h.tensor_copy(
                    A[0:65, CSH_off + hh: CSH_off + hh + 1],
                    bank(4 + hh)[0:65, 0:1]),
                    waits=[mcs]))
            m_cs = ("vector", s.tot["vector"])
            ep_done = {}
            pending_ep = [None]
            for qg in range(4):
                u_done, usplit_done, av_done = {}, {}, {}

                def emit_qk(kc, qg=qg):
                    wts = []
                    for hh in range(2):
                        if kc >= 2:
                            wts.append(u_done[(hh, kc - 2)])
                    if kc == 0:
                        wts.append(m_cs)
                    for pi, (kp, qp) in enumerate(((KPH, QPH), (KPH, QPL), (KPL, QPH))):
                        for hh in range(2):
                            sb_ = bank(2 * (kc % 2) + hh)
                            tn(lambda h, sb_=sb_, kc=kc, hh=hh, qg=qg, kp=kp, qp=qp,
                               pi=pi: h.matmul(
                                sb_, kp[64 * hh:64 * hh + 64, kc * 128:kc * 128 + 128],
                                qp[64 * hh:64 * hh + 64, qg * 512:qg * 512 + 512],
                                start=(pi == 0), stop=(pi == 2)),
                               waits=wts if (pi == 0 and hh == 0) else [])
                    qk = ("tensor", s.tot["tensor"])
                    for hh in range(2):
                        ui = (kc % 4) * 2 + hh
                        uw = [qk]
                        if kc >= 4:
                            uw.append(av_done[(1, kc - 4)])
                        u_done[(hh, kc)] = sc(lambda h, kc=kc, hh=hh, ui=ui: h.activation(
                            U32[ui], bank(2 * (kc % 2) + hh), AF.Square,
                            bias=cpc("BETA"), scale=cpc("ALPHA")), waits=uw)
                        uhw = [u_done[(hh, kc)]] + ([av_done[(1, kc - 4)]] if kc >= 4 else [])
                        if kc % 2 == 1:
                            sc(lambda h, ui=ui: h.activation(
                                UH[ui][:, :], U32[ui], AF.Identity), waits=uhw)
                            muh = ("scalar", s.tot["scalar"])
                            usplit_done[(hh, kc)] = v(
                                lambda h, ui=ui: h.tensor_tensor(
                                    UL[ui][:, :], U32[ui], UH[ui][:, :], ALU.subtract),
                                waits=[muh])
                            continue
                        v(lambda h, ui=ui: h.tensor_copy(UH[ui][:, :], U32[ui]),
                          waits=uhw)
                        usplit_done[(hh, kc)] = v(lambda h, ui=ui: h.tensor_tensor(
                            UL[ui][:, :], U32[ui], UH[ui][:, :], ALU.subtract))

                def emit_av(kc, qg=qg):
                    for pi, (vp, up) in enumerate(((VPH, UH), (VPH, UL), (VPL, UH))):
                        for hh in range(2):
                            avb = bank(4 + 2 * (qg % 2) + hh)
                            ui = (kc % 4) * 2 + hh
                            avw = []
                            if pi == 0:
                                avw = [usplit_done[(hh, kc)]]
                                if kc == 0 and hh == 0:
                                    if qg == 0:
                                        avw += cs_ev
                                    if qg >= 2:
                                        avw.append(ep_done[(0, qg - 2)])
                                        avw.append(ep_done[(1, qg - 2)])
                            tn(lambda h, kc=kc, hh=hh, avb=avb, vp=vp, up=up, ui=ui,
                               pi=pi: h.matmul(
                                avb[0:66, :],
                                vp[:, kc * 132 + 66 * hh: kc * 132 + 66 * hh + 66],
                                up[ui][:, :],
                                start=(kc == 0 and pi == 0),
                                stop=(kc == NT - 1 and pi == 2)),
                               waits=avw)
                    for hh in range(2):
                        av_done[(hh, kc)] = ("tensor", s.tot["tensor"])

                for kc in range(NT):
                    emit_qk(kc)
                    if kc == 6 and pending_ep[0] is not None:
                        pending_ep[0]()
                        pending_ep[0] = None
                    if kc >= 2:
                        emit_av(kc - 2)
                emit_av(NT - 2)
                emit_av(NT - 1)
                m_av = ("tensor", s.tot["tensor"])
                qp_ = qg % 2
                mav2s = []
                for hh in range(2):
                    avb = bank(4 + 2 * qp_ + hh)
                    v(lambda h, avb=avb, hh=hh, qp_=qp_: h.tensor_scalar(
                        AVS2[qp_][hh][0:65, :], avb[0:65, :],
                        A[0:65, CSH_off + hh: CSH_off + hh + 1], None, ALU.add),
                        waits=[m_av])
                    mav2s.append(v(lambda h, hh=hh, qp_=qp_: h.reciprocal(
                        RDEN2[qp_][hh], AVS2[qp_][hh][64:65, :])))

                def make_ep(qg=qg, qp_=qp_, mav2s=tuple(mav2s)):
                  def run():
                    for hh in range(2):
                        avb = bank(4 + 2 * qp_ + hh)
                        tn(lambda h, avb=avb, hh=hh, qp_=qp_: h.matmul(
                            avb[0:64, :], ONESM[64:65, 0:64],
                            RDEN2[qp_][hh], start=True, stop=True),
                           waits=[mav2s[hh]])
                        m_rr = ("tensor", s.tot["tensor"])
                        ep_done[(hh, qg)] = v(lambda h, hh=hh, qg=qg, qp_=qp_,
                                              avb=avb: h.tensor_tensor(
                            A[0:64, WV_off + hh * 2048 + qg * 512:
                              WV_off + hh * 2048 + qg * 512 + 512],
                            AVS2[qp_][hh][0:64, :], avb[0:64, :], ALU.mult),
                            waits=[m_rr])
                  return run
                pending_ep[0] = make_ep()
            pending_ep[0]()
            pending_ep[0] = None
            m_wv = ("vector", s.tot["vector"])
            wv_spill[p_] = s.dma(
                "gpsimd",
                ap(wv3_d, p_ * 128 * 2048, [[2048, 64], [131072, 2], [1, 2048]]),
                ap(A, WV_off, [[W, 64], [2048, 2], [1, 2048]]),
                waits=[m_wv])
            s.barrier()

        WOS = A[:, 0:8192]
        WVB = [A[:, 8192:9216], A[:, 9216:10240]]
        XNB = [A[:, 10240:11264], A[:, 11264:12288]]
        XB4 = [A[:, 12288:13312], A[:, 13312:14336]]
        OT = [A[:, 14336:15360], A[:, 15360:16384]]
        OTMP = A[:, 16384:16896]
        dwo = s.dma("sync", WOS[:, :],
                    ap(wo_d, 0, [[1024, 128], [131072, 8], [1, 1024]]))
        t4_end = {}
        for tto in range(NTO):
            par = tto % 2
            wprev = t4_end.get(tto - 2, [])
            dtoks = []
            for pp in range(8):
                dtoks.append(s.dma(
                    "sync", ap(A, 8192 + par * 1024 + pp * 128, [[W, 128], [1, 128]]),
                    ap(wv3_d, pp * 128 * 2048 + tto * 128, [[2048, 128], [1, 128]]),
                    waits=wprev))
            dtoks.append(s.dma(
                "sync", XNB[par],
                ap(xt3_d, tto * D * 128, [[128, 128], [16384, 8], [1, 128]]),
                waits=wprev))
            dtoks.append(s.dma("sync", XB4[par], x_d[tto * 128:(tto + 1) * 128, :],
                               waits=wprev))
            for nt in range(2):
                pb = bank((tto * 2 + nt) % 4)
                for dc in range(8):
                    tn(lambda h, pb=pb, dc=dc, par=par, nt=nt: h.matmul(
                        pb, WVB[par][:, dc * 128:dc * 128 + 128],
                        WOS[:, dc * 1024 + nt * 512: dc * 1024 + nt * 512 + 512],
                        start=(dc == 0), stop=False),
                       waits=(wprev + dtoks + [dwo]) if dc == 0 else [])
                for dc in range(8):
                    tn(lambda h, pb=pb, dc=dc, par=par, nt=nt: h.matmul(
                        pb, XNB[par][:, dc * 128:dc * 128 + 128],
                        MBUF[:, dc * 1024 + nt * 512: dc * 1024 + nt * 512 + 512],
                        start=False, stop=(dc == 7)))
                mo = ("tensor", s.tot["tensor"])
                v(lambda h, pb=pb, par=par, nt=nt: h.tensor_tensor(
                    OTMP[:, :], pb, XB4[par][:, nt * 512:nt * 512 + 512], ALU.add),
                  waits=[mo, dtoks[-1]] + wprev)
                v(lambda h, par=par, nt=nt: h.tensor_tensor(
                    OT[par][:, nt * 512:nt * 512 + 512], OTMP[:, :],
                    BIAR[:, nt * 512:nt * 512 + 512], ALU.add))
            mot = ("vector", s.tot["vector"])
            s.dma("gpsimd", out_d[tto * 128:(tto + 1) * 128, :], OT[par],
                  waits=[mot] + wprev)
            t4_end[tto] = s.snapshot()

        s.add("vector", lambda h: h.memset(ONESM[0:1, 0:1], 1.0),
              waits=s.snapshot())

        def make_body(ops):
            def body(h):
                last_wait = {}
                for fn, waits, semkey, inc in ops:
                    for (tk, tv) in waits:
                        if last_wait.get(tk, -1) < tv:
                            h.wait_ge(sems[tk], tv)
                            last_wait[tk] = tv
                    inst = fn(h)
                    inst.then_inc(sems[semkey], inc)
            return body

        for en in ENGS:
            if s.ops[en]:
                getattr(blk, en)(make_body(s.ops[en]))

    return nc


def _host_const_arrays():
    cpool = _host_cpool()
    pvals = np.arange(128, dtype=np.float32).reshape(128, 1)
    L = np.arange(32, dtype=np.float64) * (math.log(21.0) / 31.0)
    lhi = np.tile(L.astype(np.float32), (128, 1))
    llo = np.tile((L - L.astype(np.float32).astype(np.float64)).astype(np.float32),
                  (128, 1))
    ident = np.eye(128, dtype=np.float32)
    m = np.outer(np.arange(1024, dtype=np.int64), np.arange(1024, dtype=np.int64)) % 1024
    ang = 2.0 * np.pi * m.astype(np.float64) / 1024.0
    ctab = _wt_layout(np.cos(ang).astype(np.float32), 512)
    stab = _wt_layout(np.sin(ang).astype(np.float32), 512)
    return cpool, pvals, lhi, llo, ident, ctab, stab


def _wt_layout(w, jw):
    r = w.reshape(8, 128, 1024 // jw, jw)
    return np.ascontiguousarray(r.transpose(2, 1, 0, 3)).reshape(w.shape)


_PROG_CACHE = {}

def _get_prog(debug=False):
    if debug not in _PROG_CACHE:
        _PROG_CACHE[debug] = build_program(debug)
    return _PROG_CACHE[debug]


def kernel(L0_attn_x, L0_attn_ln_w, L0_attn_ln_b, L0_attn_w_q, L0_attn_w_kv,
           L0_attn_w_o, L0_attn_theta, L0_ffn_w_freq_real, L0_ffn_w_freq_imag,
           L0_ffn_bias, _debug=False, _cores=8):
    x = np.ascontiguousarray(np.asarray(L0_attn_x, dtype=np.float32))
    cpool, pvals, lhi, llo, ident, ctab, stab = _host_const_arrays()
    common = dict(
        wq=np.ascontiguousarray(np.asarray(L0_attn_w_q, np.float32)),
        wkv=np.ascontiguousarray(np.asarray(L0_attn_w_kv, np.float32)),
        wo=np.ascontiguousarray(np.asarray(L0_attn_w_o, np.float32)),
        lnw=np.asarray(L0_attn_ln_w, np.float32).reshape(1, -1),
        lnb=np.asarray(L0_attn_ln_b, np.float32).reshape(1, -1),
        fbias=np.asarray(L0_ffn_bias, np.float32).reshape(1, -1),
        wfr=_wt_layout(np.asarray(L0_ffn_w_freq_real, np.float32), 128),
        wfi=_wt_layout(np.asarray(L0_ffn_w_freq_imag, np.float32), 128),
        ctab=ctab, stab=stab,
        theta=np.tile(np.asarray(L0_attn_theta, np.float32).reshape(1, 1), (128, 1)),
        cpool=cpool, pvals=pvals, lhi=lhi, llo=llo, ident=ident,
    )
    in_maps = []
    for c in range(_cores):
        b, j = c // 2, c % 2
        xr = np.roll(x[b], -2048 * j, axis=0)
        tvals = (128.0 * ((np.arange(32) + 16 * j) % 32)).astype(np.float32).reshape(32, 1)
        im = dict(common)
        im["x"] = np.ascontiguousarray(xr)
        im["t128vals"] = tvals
        in_maps.append(im)
    nc = _get_prog(_debug)
    res = run_bass_kernel_spmd(nc, in_maps, core_ids=list(range(_cores)))
    out = np.empty((4, 4096, 1024), np.float32)
    for c in range(_cores):
        b, j = c // 2, c % 2
        out[b, 2048 * j:2048 * (j + 1), :] = res.results[c]["out"]
    if _debug:
        return out, res
    return out


# version 18
# speedup vs baseline: 1.0632x; 1.0632x over previous
import math
import numpy as np
from contextlib import ExitStack

import concourse.bass as bass
import concourse.mybir as mybir
from concourse.bass_utils import run_bass_kernel_spmd

F32 = mybir.dt.float32
BF16 = mybir.dt.bfloat16
AF = mybir.ActivationFunctionType
ALU = mybir.AluOpType
AX = mybir.AxisListType

ENGS = ("sync", "gpsimd", "tensor", "vector", "scalar")
W = 35840
D = 1024
T = 4096
NT = 32
NTO = 16
NP = 8

def _cp_def():
    c = {}
    def put(name, val):
        c[name] = (len(c), float(val))
    put("ONE", 1.0); put("HALF", 0.5); put("EPS", 1e-5)
    put("INV1024", 1.0 / 1024); put("NINV1024", -1.0 / 1024)
    hi = np.float32(0.2); put("C02HI", hi); put("C02LO", np.float32(0.2 - np.float64(hi)))
    put("NEG1", -1.0)
    put("ALPHA", 1.0 / (8.0 * math.sqrt(2.0))); put("BETA", 1.0 / math.sqrt(2.0))
    put("PI", math.pi); put("PIH", math.pi / 2)
    put("NR15", 1.5); put("NRN05", -0.5)
    hi = np.float32(1.0 / 220.0); put("I220HI", hi)
    put("I220LO", np.float32(1.0 / 220.0 - np.float64(hi)))
    for k in range(17):
        v = 2.0 * math.pi * (2.0 ** k)
        h = np.float32(v); l = np.float32(v - np.float64(h))
        put(f"KHI{k}", h); put(f"NKHI{k}", -h); put(f"NKLO{k}", -l)
    return c
_CP = _cp_def()
NCPOOL = 128


def _host_cpool():
    a = np.zeros((128, NCPOOL), np.float32)
    for name, (i, v) in _CP.items():
        a[:, i] = np.float32(v)
    return a


class Sched:
    NSL = {"sync": 16, "gpsimd": 6}

    def __init__(self):
        self.keys = ["tensor", "vector", "scalar", "gpsimdc"]
        for e_ in ("sync", "gpsimd"):
            self.keys += [f"{e_}{i}" for i in range(self.NSL[e_])]
        self.ops = {e_: [] for e_ in ENGS}
        self.tot = {k: 0 for k in self.keys}
        self.didx = {"sync": 0, "gpsimd": 0}
        self.pending = {e_: {} for e_ in ENGS}

    def add(self, eng, fn, waits=(), semkey=None):
        if semkey is None:
            semkey = eng
        wd = dict(self.pending[eng]); self.pending[eng] = {}
        if eng in ("vector", "scalar") and self.tot[eng] > 0:
            wd[eng] = self.tot[eng]
        for (tk, tv) in waits:
            if wd.get(tk, -1) < tv:
                wd[tk] = tv
        inc = 16 if (eng in ("sync", "gpsimd") and semkey != "gpsimdc") else 1
        self.ops[eng].append((fn, tuple(sorted(wd.items())), semkey, inc))
        self.tot[semkey] += inc
        return (semkey, self.tot[semkey])

    def dma(self, eng, out, in_, waits=()):
        i = self.didx[eng]; self.didx[eng] += 1
        key = f"{eng}{i % self.NSL[eng]}"
        w = list(waits)
        if self.tot[key]:
            w.append((key, self.tot[key]))
        return self.add(eng, lambda h, o=out, i_=in_: h.dma_start(out=o, in_=i_),
                        w, semkey=key)

    def snapshot(self):
        return [(k, v_) for k, v_ in self.tot.items() if v_ > 0]

    def barrier(self):
        snap = self.snapshot()
        for e_ in ENGS:
            for (k, v_) in snap:
                if self.pending[e_].get(k, -1) < v_:
                    self.pending[e_][k] = v_
        return snap


def ap(t, off, dims):
    return bass.AP(t, off, dims)


def build_program(debug=False):
    nc = bass.Bass("TRN2", target_bir_lowering=False)

    x_d = nc.dram_tensor("x", [T, D], F32, kind="ExternalInput")
    wq_d = nc.dram_tensor("wq", [D, D], F32, kind="ExternalInput")
    wkv_d = nc.dram_tensor("wkv", [D, 2 * D], F32, kind="ExternalInput")
    wo_d = nc.dram_tensor("wo", [D, D], F32, kind="ExternalInput")
    lnw_d = nc.dram_tensor("lnw", [1, D], F32, kind="ExternalInput")
    lnb_d = nc.dram_tensor("lnb", [1, D], F32, kind="ExternalInput")
    fb_d = nc.dram_tensor("fbias", [1, D], F32, kind="ExternalInput")
    wfr_d = nc.dram_tensor("wfr", [D, D], F32, kind="ExternalInput")
    wfi_d = nc.dram_tensor("wfi", [D, D], F32, kind="ExternalInput")
    ct_d = nc.dram_tensor("ctab", [D, D], F32, kind="ExternalInput")
    st_d = nc.dram_tensor("stab", [D, D], F32, kind="ExternalInput")
    th_d = nc.dram_tensor("theta", [128, 1], F32, kind="ExternalInput")
    cp_d = nc.dram_tensor("cpool", [128, NCPOOL], F32, kind="ExternalInput")
    pv_d = nc.dram_tensor("pvals", [128, 1], F32, kind="ExternalInput")
    tv_d = nc.dram_tensor("t128vals", [32, 1], F32, kind="ExternalInput")
    lhi_d = nc.dram_tensor("lhi", [128, 32], F32, kind="ExternalInput")
    llo_d = nc.dram_tensor("llo", [128, 32], F32, kind="ExternalInput")
    id_d = nc.dram_tensor("ident", [128, 128], F32, kind="ExternalInput")
    out_d = nc.dram_tensor("out", [2048, D], F32, kind="ExternalOutput")
    skind = "ExternalOutput" if debug else "Internal"
    ht3_d = nc.dram_tensor("ht3", [NT, D, 128], F32, kind=skind)
    xt3_d = nc.dram_tensor("xt3", [NTO, D, 128], F32, kind=skind)
    wv3_d = nc.dram_tensor("wv3", [NP, 128, 2048], F32, kind=skind)
    if debug:
        dbg_ecos = nc.dram_tensor("dbg_ecos", [128, 1024], F32, kind="ExternalOutput")
        dbg_esin = nc.dram_tensor("dbg_esin", [128, 2048], F32, kind="ExternalOutput")
        dbg_m = nc.dram_tensor("dbg_m", [128, 8192], F32, kind="ExternalOutput")
        dbg_kp = nc.dram_tensor("dbg_kp", [128, 4096], F32, kind="ExternalOutput")
        dbg_qp = nc.dram_tensor("dbg_qp", [128, 2048], F32, kind="ExternalOutput")
        dbg_vp = nc.dram_tensor("dbg_vp", [128, 4224], F32, kind="ExternalOutput")
        dbg_sm = nc.dram_tensor("dbg_sm", [128, 512], F32, kind="ExternalOutput")
        dbg_sm2 = nc.dram_tensor("dbg_sm2", [32, 256], F32, kind="ExternalOutput")
        dbg_xn = nc.dram_tensor("dbg_xn", [128, 1024], F32, kind="ExternalOutput")
        dbg_hh = nc.dram_tensor("dbg_hh", [128, 1024], F32, kind="ExternalOutput")
        dbg_st = nc.dram_tensor("dbg_st", [128, 48], F32, kind="ExternalOutput")

    s = Sched()
    with ExitStack() as ctx:
        e = ctx.enter_context
        A = e(nc.sbuf_tensor("A", [128, W], F32))
        CPO = e(nc.sbuf_tensor("CPO", [128, NCPOOL], F32))
        PV = e(nc.sbuf_tensor("PV", [128, 1], F32))
        TV = e(nc.sbuf_tensor("TV", [32, 1], F32))
        LHI = e(nc.sbuf_tensor("LHI", [128, 32], F32))
        LLO = e(nc.sbuf_tensor("LLO", [128, 32], F32))
        IDN = e(nc.sbuf_tensor("IDN", [128, 128], F32))
        TH = e(nc.sbuf_tensor("TH", [128, 1], F32))
        ONESM = e(nc.sbuf_tensor("ONESM", [128, 128], F32))
        LNWR = e(nc.sbuf_tensor("LNWR", [128, D], F32))
        LNBR = e(nc.sbuf_tensor("LNBR", [128, D], F32))
        BIAR = e(nc.sbuf_tensor("BIAR", [128, D], F32))
        ECOS = e(nc.sbuf_tensor("ECOS", [128, 1024], F32))
        ESIN = e(nc.sbuf_tensor("ESIN", [128, 2048], F32))
        MBUF = e(nc.sbuf_tensor("MBUF", [128, 8192], F32))
        SM = e(nc.sbuf_tensor("SM", [128, 512], F32))
        SM2 = e(nc.sbuf_tensor("SM2", [32, 256], F32))
        WROW = A[0:1, 2048:5120]
        C1024 = A[0:1, 5120:6144]
        S1024 = A[0:1, 6144:7168]
        P2 = [e(nc.psum_tensor(f"P2_{i}", [128, 1024], F32)) for i in range(4)]
        sems = {k: e(nc.semaphore(f"sem_{k}")) for k in s.keys}
        blk = e(nc.Block())

        def bank(i):
            return P2[i // 2][:, (i % 2) * 512:(i % 2) * 512 + 512]

        def cpc(name):
            i = _CP[name][0]
            return CPO[:, i:i + 1]

        def cpc_n(name, n):
            i = _CP[name][0]
            return CPO[0:n, i:i + 1]

        def v(fn, waits=()):
            return s.add("vector", fn, waits)

        def sc(fn, waits=()):
            return s.add("scalar", fn, waits)

        def tn(fn, waits=()):
            return s.add("tensor", fn, waits)

        def reduce_chain(g, scr, kmax, npart):
            for k in range(kmax, -1, -1):
                v(lambda h, g=g, scr=scr, k=k, n=npart: h.tensor_scalar(
                    scr, g, cpc_n(f"KHI{k}", n), None, ALU.is_ge))
                v(lambda h, g=g, scr=scr, k=k, n=npart: h.scalar_tensor_tensor(
                    g, scr, cpc_n(f"NKHI{k}", n), g, ALU.mult, ALU.add))
                v(lambda h, g=g, scr=scr, k=k, n=npart: h.scalar_tensor_tensor(
                    g, scr, cpc_n(f"NKLO{k}", n), g, ALU.mult, ALU.add))

        def wrap_pm_pi(g, scr, npart):
            v(lambda h, g=g, scr=scr, n=npart: h.tensor_scalar(
                scr, g, cpc_n("PI", n), None, ALU.is_ge))
            v(lambda h, g=g, scr=scr, n=npart: h.scalar_tensor_tensor(
                g, scr, cpc_n("NKHI0", n), g, ALU.mult, ALU.add))
            v(lambda h, g=g, scr=scr, n=npart: h.scalar_tensor_tensor(
                g, scr, cpc_n("NKLO0", n), g, ALU.mult, ALU.add))

        s.dma("sync", CPO[:, :], cp_d[:, :])
        d_pv = s.dma("sync", PV[:, :], pv_d[:, :])
        d_tv = s.dma("sync", TV[:, :], tv_d[:, :])
        s.dma("sync", LHI[:, :], lhi_d[:, :])
        d_llo = s.dma("sync", LLO[:, :], llo_d[:, :])
        s.dma("sync", IDN[:, :], id_d[:, :])
        d_th = s.dma("sync", TH[:, :], th_d[:, :])
        s.dma("sync", WROW[:, 0:D], lnw_d[:, :])
        s.dma("sync", WROW[:, D:2 * D], lnb_d[:, :])
        d_w3 = s.dma("sync", WROW[:, 2 * D:3 * D], fb_d[:, :])
        d_ph0 = s.snapshot()

        m_ones = v(lambda h: h.memset(ONESM[:, :], 1.0))
        v(lambda h: h.memset(SM[:, :], 0.0))
        v(lambda h: h.memset(SM2[:, :], 0.0))

        reps = []
        for i, (so, dst) in enumerate(((0, LNWR), (D, LNBR), (2 * D, BIAR))):
            pt = P2[2 + (i % 2)]
            w_extra = [reps[i - 2]] if i >= 2 else []
            for half in range(2):
                tn(lambda h, pt=pt, half=half, so=so: h.matmul(
                    pt[:, half * 512:half * 512 + 512], ONESM[0:1, :],
                    WROW[:, so + half * 512: so + half * 512 + 512],
                    start=True, stop=True),
                   waits=d_ph0 + [m_ones] + w_extra)
            mm_done = ("tensor", s.tot["tensor"])
            reps.append(v(lambda h, pt=pt, dst=dst: h.tensor_copy(dst[:, :], pt[:, :]),
                          waits=[mm_done]))

        Lsum = SM[:, 0:32]; evec = SM[:, 32:64]; fbv = SM[:, 64:96]
        th220 = SM[:, 96:97]; fbv2 = SM[:, 160:192]
        FRR = SM[:, 192:224]
        m = v(lambda h: h.tensor_tensor(Lsum, LHI[:, :], LLO[:, :], ALU.add),
              waits=d_ph0)
        m = sc(lambda h: h.activation(evec, Lsum, AF.Exp), waits=[m] + d_ph0)
        v(lambda h: h.tensor_scalar(fbv, evec, cpc("ONE"), None, ALU.subtract),
          waits=[m])
        v(lambda h: h.tensor_scalar(fbv2, fbv, cpc("C02HI"), None, ALU.mult))
        v(lambda h: h.scalar_tensor_tensor(fbv2, fbv, cpc("C02LO"), fbv2,
                                           ALU.mult, ALU.add))
        v(lambda h: h.tensor_scalar(th220, TH[:, :], cpc("I220HI"), None, ALU.mult))
        v(lambda h: h.scalar_tensor_tensor(th220, TH[:, :], cpc("I220LO"), th220,
                                           ALU.mult, ALU.add))
        m_frr = v(lambda h: h.tensor_scalar(FRR, fbv2, ap(SM, 96, [[512, 128], [1, 1]]),
                                            None, ALU.mult))

        ANGP = SM[:, 224:256]; SCR = SM[:, 256:288]; GC = SM[:, 288:320]
        CP_ = SM[:, 320:352]; SP_ = SM[:, 352:384]
        v(lambda h: h.tensor_scalar(ANGP, FRR, PV[:, :], None, ALU.mult))
        reduce_chain(ANGP, SCR, 11, 128)
        v(lambda h: h.tensor_scalar(GC, ANGP, cpc("PIH"), None, ALU.add))
        wrap_pm_pi(GC, SCR, 128)
        m = ("vector", s.tot["vector"])
        m_cp = sc(lambda h: h.activation(CP_, GC, AF.Sin), waits=[m])
        wrap_pm_pi(ANGP, SCR, 128)
        m = ("vector", s.tot["vector"])
        m_ps = sc(lambda h: h.activation(SP_, ANGP, AF.Sin), waits=[m])

        ANGT = SM2[:, 0:32]; SCR2 = SM2[:, 32:64]; GC2 = SM2[:, 64:96]
        CT_ = SM2[:, 96:128]; ST_ = SM2[:, 128:160]
        v(lambda h: h.tensor_scalar(ANGT, FRR[0:32, :], TV[:, :], None, ALU.mult),
          waits=[m_frr])
        reduce_chain(ANGT, SCR2, 16, 32)
        v(lambda h: h.tensor_scalar(GC2, ANGT, cpc_n("PIH", 32), None, ALU.add))
        wrap_pm_pi(GC2, SCR2, 32)
        m = ("vector", s.tot["vector"])
        m_ct = sc(lambda h: h.activation(CT_, GC2, AF.Sin), waits=[m])
        wrap_pm_pi(ANGT, SCR2, 32)
        m = ("vector", s.tot["vector"])
        m_ts = sc(lambda h: h.activation(ST_, ANGT, AF.Sin), waits=[m])

        g1 = s.dma("gpsimd", C1024, CT_, waits=[m_ct, m_ts])
        g2 = s.dma("gpsimd", S1024, ST_, waits=[m_ct, m_ts])
        for half in range(2):
            tn(lambda h, half=half: h.matmul(
                P2[0][:, half * 512:half * 512 + 512], ONESM[0:1, :],
                C1024[:, half * 512:half * 512 + 512], start=True, stop=True),
               waits=[g1])
            tn(lambda h, half=half: h.matmul(
                P2[1][:, half * 512:half * 512 + 512], ONESM[0:1, :],
                S1024[:, half * 512:half * 512 + 512], start=True, stop=True),
               waits=[g2])
        m_rep = ("tensor", s.tot["tensor"])

        CPV = ap(SM, 320, [[512, 128], [0, 32], [1, 32]])
        SPV = ap(SM, 352, [[512, 128], [0, 32], [1, 32]])
        TA = A[:, 0:1024]
        TAv = ap(A, 0, [[W, 128], [32, 32], [1, 32]])
        TBv = ap(A, 1024, [[W, 128], [32, 32], [1, 32]])
        P0v = ap(P2[0], 0, [[1024, 128], [32, 32], [1, 32]])
        P1v = ap(P2[1], 0, [[1024, 128], [32, 32], [1, 32]])
        ECv = ap(ECOS, 0, [[1024, 128], [32, 32], [1, 32]])

        v(lambda h: h.tensor_tensor(TAv, P0v, CPV, ALU.mult),
          waits=[m_rep, m_cp, m_ps])
        v(lambda h: h.tensor_tensor(TBv, P1v, SPV, ALU.mult))
        v(lambda h: h.tensor_tensor(ECv, TAv, TBv, ALU.subtract))
        v(lambda h: h.tensor_tensor(TAv, P1v, CPV, ALU.mult))
        v(lambda h: h.tensor_tensor(TBv, P0v, SPV, ALU.mult))
        v(lambda h: h.tensor_tensor(TAv, TAv, TBv, ALU.add))
        v(lambda h: h.tensor_scalar(ap(ESIN, 0, [[2048, 128], [2, 1024]]),
                                    TA[:, :], cpc("NEG1"), None, ALU.mult))
        mes = v(lambda h: h.tensor_scalar(ap(ESIN, 1, [[2048, 128], [2, 1024]]),
                                          TA[:, :], cpc("ONE"), None, ALU.mult))
        if debug:
            s.dma("gpsimd", dbg_ecos[:, :], ECOS[:, :], waits=[mes])
            s.dma("gpsimd", dbg_esin[:, :], ESIN[:, :], waits=[mes])
            s.dma("gpsimd", dbg_sm[:, :], SM[:, :], waits=[mes])
            s.dma("gpsimd", dbg_sm2[:, :], SM2[:, :], waits=[mes])

        s.barrier()

        CB, SB2 = A[:, 0:4096], A[:, 4096:8192]
        WRS, WIS, WIN = A[:, 8192:9216], A[:, 9216:10240], A[:, 10240:11264]
        PRT_off, PIT_off = 11264, 19456
        WR2 = [WRS, MBUF[:, 0:1024], MBUF[:, 1024:2048], MBUF[:, 2048:3072]]
        WI2 = [WIS, MBUF[:, 3072:4096], MBUF[:, 4096:5120], MBUF[:, 5120:6144]]
        WN2 = [WIN, MBUF[:, 6144:7168]]

        def cs_block(col0, waits):
            dt_ = col0 // 512
            t1 = s.dma("sync", CB[:, :],
                       ap(ct_d, dt_ * 524288, [[4096, 128], [1, 4096]]), waits=waits)
            t2 = s.dma("gpsimd", SB2[:, :],
                       ap(st_d, dt_ * 524288, [[4096, 128], [1, 4096]]), waits=waits)
            return [t1, t2]

        p1s = {"ev_prev": None, "dcs": None, "mprev": None, "sc_ev": {},
               "pend_evac": None, "mm_hist": {}}

        def ph1_mms(i):
            if i >= 32:
                return
            if i < 16:
                dt, oc = i // 8, i % 8
                sl_ = i % 4
                sn_ = i % 2
                if oc == 0:
                    p1s["mprev"] = ("tensor", s.tot["tensor"])
                    p1s["dcs"] = cs_block(dt * 512, [p1s["mprev"]])
                mprev, dcs = p1s["mprev"], p1s["dcs"]
                wpre = [p1s["mm_hist"][i - 4]] if i >= 4 else []
                dwr = s.dma("sync", WR2[sl_][:, :],
                            ap(wfr_d, oc * 131072, [[1024, 128], [1, 1024]]),
                            waits=wpre)
                dwi = s.dma("gpsimd", WI2[sl_][:, :],
                            ap(wfi_d, oc * 131072, [[1024, 128], [1, 1024]]),
                            waits=wpre)
                mneg = sc(lambda h, sl_=sl_, sn_=sn_: h.activation(
                    WN2[sn_][:, :], WI2[sl_][:, :], AF.Identity, scale=cpc("NEG1")),
                          waits=[dwi] + ([p1s["mm_hist"][i - 2]] if i >= 2 else []))
                bpr, bpi = bank(0 + (oc % 2)), bank(2 + (oc % 2))
                w0 = [dwr, dwi] + dcs + ([p1s["ev_prev"]] if p1s["ev_prev"] else [])
                for fc in range(8):
                    tn(lambda h, b=bpr, fc=fc, sl_=sl_: h.matmul(
                        b, WR2[sl_][:, fc * 128:fc * 128 + 128],
                        CB[:, fc * 512:fc * 512 + 512], start=(fc == 0), stop=False),
                       waits=w0 if fc == 0 else [])
                    tn(lambda h, b=bpr, fc=fc, sl_=sl_: h.matmul(
                        b, WI2[sl_][:, fc * 128:fc * 128 + 128],
                        SB2[:, fc * 512:fc * 512 + 512], start=False, stop=(fc == 7)))
                for fc in range(8):
                    tn(lambda h, b=bpi, fc=fc, sl_=sl_: h.matmul(
                        b, WR2[sl_][:, fc * 128:fc * 128 + 128],
                        SB2[:, fc * 512:fc * 512 + 512], start=(fc == 0), stop=False),
                       waits=[mneg] if fc == 0 else [])
                    tn(lambda h, b=bpi, fc=fc, sn_=sn_: h.matmul(
                        b, WN2[sn_][:, fc * 128:fc * 128 + 128],
                        CB[:, fc * 512:fc * 512 + 512], start=False, stop=(fc == 7)))
                p1s["mprev"] = ("tensor", s.tot["tensor"])
                p1s["mm_hist"][i] = p1s["mprev"]
                pr_dst = A[:, PRT_off + oc * 1024 + dt * 512:
                           PRT_off + oc * 1024 + dt * 512 + 512]
                pi_dst = A[:, PIT_off + oc * 1024 + dt * 512:
                           PIT_off + oc * 1024 + dt * 512 + 512]
                p1s["pend_evac"] = (bpr, pr_dst, bpi, pi_dst, p1s["mprev"])
            else:
                j = i - 16
                mt, dc = j // 8, j % 8
                if dc == 0:
                    p1s["mprev"] = ("tensor", s.tot["tensor"])
                    p1s["dcs"] = cs_block(mt * 512, [p1s["mprev"]])
                b = bank(4 + (dc % 2))
                li = mt * 8 + dc
                wts = list(p1s["dcs"]) + [p1s["ev_prev"]]
                if li >= 2:
                    wts.append(p1s["sc_ev"][li - 2])
                for oc in range(8):
                    tn(lambda h, b=b, oc=oc, dc=dc: h.matmul(
                        b, A[:, PRT_off + oc * 1024 + dc * 128:
                             PRT_off + oc * 1024 + dc * 128 + 128],
                        CB[:, oc * 512:oc * 512 + 512], start=(oc == 0), stop=False),
                       waits=wts if oc == 0 else [])
                    tn(lambda h, b=b, oc=oc, dc=dc: h.matmul(
                        b, A[:, PIT_off + oc * 1024 + dc * 128:
                             PIT_off + oc * 1024 + dc * 128 + 128],
                        SB2[:, oc * 512:oc * 512 + 512], start=False, stop=(oc == 7)))
                mdone = ("tensor", s.tot["tensor"])
                p1s["sc_ev"][li] = sc(
                    lambda h, b=b, dc=dc, mt=mt: h.activation(
                        MBUF[:, dc * 1024 + mt * 512: dc * 1024 + mt * 512 + 512], b,
                        AF.Identity, scale=cpc("INV1024")),
                    waits=[mdone])

        def ph1_evacs(i):
            if p1s["pend_evac"] is None:
                return
            bpr, pr_dst, bpi, pi_dst, mprev = p1s["pend_evac"]
            p1s["pend_evac"] = None
            sc(lambda h, b=bpr, dst=pr_dst: h.activation(dst, b, AF.Identity),
               waits=[mprev])
            p1s["ev_prev"] = sc(lambda h, b=bpi, dst=pi_dst: h.activation(
                dst, b, AF.Identity))

        XB = [A[:, 27648:28672], A[:, 28672:29696]]
        D1_ = [A[:, 29696:30720], A[:, 30720:31744]]
        D2_ = [A[:, 31744:32768], A[:, 32768:33792]]
        XN_ = [A[:, 33792:34816], A[:, 34816:35840]]
        SQS = XB
        HH_ = XB
        XT = D1_
        HT = D2_

        def stat(par, i):
            o = 384 + par * 24 + i
            return SM[:, o:o + 1]

        def ln_stats(par, i0, src, dst, w0):
            m2 = v(lambda h, p=par, i0=i0, src=src: (
                h.reduce_sum(stat(p, i0), src, axis=AX.X)), waits=w0)
            m2 = v(lambda h, p=par, i0=i0: h.tensor_scalar(
                stat(p, i0 + 1), stat(p, i0), cpc("NINV1024"), None, ALU.mult))
            sc(lambda h, p=par, i0=i0, src=src, dst=dst: h.activation(
                dst, src, AF.Identity, bias=stat(p, i0 + 1)), waits=[m2] + w0)
            m3 = sc(lambda h, p=par, i0=i0, dst=dst: h.activation(
                SQS[p], dst, AF.Square, accum_out=stat(p, i0 + 2)))
            m4 = v(lambda h, p=par, i0=i0: h.tensor_scalar(
                stat(p, i0 + 3), stat(p, i0 + 2), cpc("INV1024"), None, ALU.mult),
                waits=[m3])
            m5 = sc(lambda h, p=par, i0=i0: h.activation(
                stat(p, i0 + 4), stat(p, i0 + 3), AF.Sqrt, bias=cpc("EPS")),
                waits=[m4])
            v(lambda h, p=par, i0=i0: h.reciprocal(stat(p, i0 + 5), stat(p, i0 + 4)),
              waits=[m5])
            v(lambda h, p=par, i0=i0: h.tensor_scalar(
                stat(p, i0 + 6), stat(p, i0 + 3), cpc("EPS"), None, ALU.add))
            for _ in range(2):
                v(lambda h, p=par, i0=i0: h.tensor_tensor(
                    stat(p, i0 + 7), stat(p, i0 + 5), stat(p, i0 + 5), ALU.mult))
                v(lambda h, p=par, i0=i0: h.tensor_tensor(
                    stat(p, i0 + 7), stat(p, i0 + 7), stat(p, i0 + 6), ALU.mult))
                v(lambda h, p=par, i0=i0: h.tensor_scalar(
                    stat(p, i0 + 7), stat(p, i0 + 7), cpc("NRN05"), cpc("NR15"),
                    ALU.mult, ALU.add))
                v(lambda h, p=par, i0=i0: h.tensor_tensor(
                    stat(p, i0 + 5), stat(p, i0 + 5), stat(p, i0 + 7), ALU.mult))
            return ("vector", s.tot["vector"])

        tile_end = {}
        ph2m = {}

        def ph2_ln(tt):
            par = tt % 2
            wprev = tile_end.get(tt - 2, [])
            dx = s.dma("sync", XB[par], x_d[tt * 128:(tt + 1) * 128, :], waits=wprev)
            mv0 = ln_stats(par, 0, XB[par], D1_[par], [dx] + wprev)
            m6 = sc(lambda h, p=par: h.activation(XN_[p], D1_[p], AF.Identity,
                                                  scale=stat(p, 5)), waits=[mv0])
            ln_stats(par, 8, XN_[par], D2_[par], [m6])
            mv1 = ("vector", s.tot["vector"])
            mh1 = sc(lambda h, p=par: h.activation(HH_[p], D2_[p], AF.Identity,
                                                   scale=stat(p, 13)), waits=[mv1])
            v(lambda h, p=par: h.tensor_tensor(HH_[p], HH_[p], LNWR[:, :], ALU.mult),
              waits=[mh1])
            mh = v(lambda h, p=par: h.tensor_tensor(HH_[p], HH_[p], LNBR[:, :],
                                                    ALU.add))
            ph2m[tt] = (mh, m6)

        last_tr_ev = [None]

        def ph2_tr(tt):
            par = tt % 2
            own = tt < NTO
            mh, m6 = ph2m[tt]
            pb = P2[3]
            w_pb = [last_tr_ev[0]] if last_tr_ev[0] else []
            for dc in range(8):
                tn(lambda h, p=par, dc=dc: h.transpose(
                    pb[:, dc * 128:dc * 128 + 128],
                    HH_[p][:, dc * 128:dc * 128 + 128], IDN[:, :]),
                   waits=([mh] + w_pb) if dc == 0 else [])
            mt1 = ("tensor", s.tot["tensor"])
            ev1 = v(lambda h, p=par: h.tensor_copy(HT[p], pb[:, :]), waits=[mt1])
            s.dma("gpsimd",
                  ap(ht3_d, tt * D * 128, [[128, 128], [16384, 8], [1, 128]]),
                  HT[par], waits=[ev1])
            last_tr_ev[0] = ev1
            if own:
                for dc in range(8):
                    tn(lambda h, p=par, dc=dc: h.transpose(
                        pb[:, dc * 128:dc * 128 + 128],
                        XN_[p][:, dc * 128:dc * 128 + 128], IDN[:, :]),
                       waits=[m6, ev1] if dc == 0 else [])
                mt2 = ("tensor", s.tot["tensor"])
                ev2 = v(lambda h, p=par: h.tensor_copy(XT[p], pb[:, :]), waits=[mt2])
                s.dma("gpsimd",
                      ap(xt3_d, tt * D * 128, [[128, 128], [16384, 8], [1, 128]]),
                      XT[par], waits=[ev2])
                last_tr_ev[0] = ev2
            tile_end[tt] = s.snapshot()

        for i in range(NT + 2):
            ph1_mms(i)
            ph1_evacs(i)
            if 2 <= i:
                ph2_tr(i - 2)
            if i < NT:
                ph2_ln(i)

        if debug:
            s.dma("gpsimd", dbg_m[:, :], MBUF[:, :],
                  waits=[("scalar", s.tot["scalar"])])
            mend = ("vector", s.tot["vector"])
            send = ("scalar", s.tot["scalar"])
            s.dma("gpsimd", dbg_xn[:, :], XN_[1], waits=[mend, send])
            s.dma("gpsimd", dbg_hh[:, :], HH_[1], waits=[mend])
            s.dma("gpsimd", dbg_st[:, :], SM[:, 384:432], waits=[mend])

        s.barrier()

        WSLAB_off = 0
        HTL = [A[:, 3072:4096], A[:, 4096:5120],
               A[:, 30000:31024], A[:, 31024:32048]]
        KPH = A[:, 5120:7168].bitcast(BF16)
        KPL = A[:, 7168:9216].bitcast(BF16)
        QPH = A[:, 9216:10240].bitcast(BF16)
        QPL = A[:, 10240:11264].bitcast(BF16)
        VPH = A[:, 11264:13376].bitcast(BF16)
        VPL = A[:, 13376:15488].bitcast(BF16)
        U32 = [A[:, 15488 + i * 512: 15488 + i * 512 + 512] for i in range(8)]
        T1_, T2_ = A[:, 19584:19712], A[:, 19712:19840]
        KR_ = [A[:, 19840:19968], A[:, 29600:29728]]
        QR_ = [A[:, 19968:20096], A[:, 29728:29856]]
        WV_off = 20096
        CSH_off = 24704
        AVS2 = [[A[:, 30000 + (2 * qp + hh) * 512: 30000 + (2 * qp + hh) * 512 + 512]
                 for hh in range(2)] for qp in range(2)]
        RDEN2 = [[A[64:65, 32048 + (2 * qp + hh) * 512: 32048 + (2 * qp + hh) * 512 + 512]
                  for hh in range(2)] for qp in range(2)]
        UH = [A[:, 25248 + i * 256: 25248 + i * 256 + 256].bitcast(BF16) for i in range(8)]
        UL = [A[:, 27296 + i * 256: 27296 + i * 256 + 256].bitcast(BF16) for i in range(8)]
        HALFB = A[:, 29900:29901].bitcast(BF16)

        wv_spill = {}

        def emit_wslab(p_, waits):
            dk = s.dma("sync", ap(A, WSLAB_off, [[W, 128], [384, 8], [1, 128]]),
                       ap(wkv_d, 128 * p_, [[2048, 128], [262144, 8], [1, 128]]),
                       waits=waits)
            dv2 = s.dma("sync", ap(A, WSLAB_off + 128, [[W, 128], [384, 8], [1, 128]]),
                        ap(wkv_d, D + 128 * p_, [[2048, 128], [262144, 8], [1, 128]]),
                        waits=waits)
            dq = s.dma("sync", ap(A, WSLAB_off + 256, [[W, 128], [384, 8], [1, 128]]),
                       ap(wq_d, 128 * p_, [[1024, 128], [131072, 8], [1, 128]]),
                       waits=waits)
            return [dk, dv2, dq]

        def emit_dh(tt, waits):
            dh = s.dma("sync", HTL[tt % 4][:, 0:512],
                       ap(ht3_d, tt * D * 128, [[128, 128], [16384, 4], [1, 128]]),
                       waits=waits)
            dh2 = s.dma("gpsimd", HTL[tt % 4][:, 512:1024],
                        ap(ht3_d, tt * D * 128 + 4 * 16384,
                           [[128, 128], [16384, 4], [1, 128]]),
                        waits=waits)
            return dh, dh2

        preload = {"dwt": None, "dh": {}}
        for p_ in range(NP):
            if preload["dwt"] is not None:
                dwt = preload["dwt"]
                pre_dh = preload["dh"]
            else:
                dwt = emit_wslab(p_, [])
                pre_dh = {}
            mhb = v(lambda h: h.memset(HALFB[:, 0:1], 0.5))
            v(lambda h: h.memset(VPH[:, :], 1.0))
            mvp1 = v(lambda h: h.memset(VPL[:, :], 0.0))
            ev_tile, ev_tile_t = {}, {}
            rope_done, rope_doneq, vcp_done, ps_free = {}, {}, {}, {}

            def emit_tr(tt):
                par = tt % 2
                own = tt < NTO
                tb = par * 512
                wv_w = [ev_tile[tt - 2]] if tt >= 2 else []
                tn(lambda h, tb=tb, par=par: h.transpose(
                    P2[3][:, tb:tb + 128], KR_[par][:, :], IDN[:, :]),
                   waits=[rope_done[tt]] + wv_w)
                mtr = ("tensor", s.tot["tensor"])
                v(lambda h, tt=tt, tb=tb: h.tensor_copy(
                    KPH[:, tt * 128:tt * 128 + 128], P2[3][:, tb:tb + 128]),
                  waits=[mtr])
                ev_k = v(lambda h, tt=tt, tb=tb: h.tensor_tensor(
                    KPL[:, tt * 128:tt * 128 + 128], P2[3][:, tb:tb + 128],
                    KPH[:, tt * 128:tt * 128 + 128], ALU.subtract))
                mtr_last = mtr
                if own:
                    tn(lambda h, tb=tb, par=par: h.transpose(
                        P2[3][:, tb + 128:tb + 256], QR_[par][:, :], IDN[:, :]),
                       waits=[rope_doneq[tt]])
                    mtr2 = ("tensor", s.tot["tensor"])
                    v(lambda h, tt=tt, tb=tb: h.tensor_copy(
                        QPH[:, tt * 128:tt * 128 + 128], P2[3][:, tb + 128:tb + 256]),
                      waits=[mtr2])
                    ev_k = v(lambda h, tt=tt, tb=tb: h.tensor_tensor(
                        QPL[:, tt * 128:tt * 128 + 128], P2[3][:, tb + 128:tb + 256],
                        QPH[:, tt * 128:tt * 128 + 128], ALU.subtract))
                    mtr_last = mtr2
                for hh2 in range(2):
                    for pl in (VPH, VPL):
                        tn(lambda h, kc=tt, hh=hh2, pl=pl: h.matmul(
                            bank(4 + hh)[0:66, 0:1],
                            pl[:, kc * 132 + 66 * hh: kc * 132 + 66 * hh + 66],
                            HALFB[:, 0:1],
                            start=(kc == 0 and pl is VPH),
                            stop=(kc == NT - 1 and pl is VPL)),
                           waits=[vcp_done[tt], mhb] if (hh2 == 0 and pl is VPH) else [])
                ev_tile[tt] = max(ev_k, vcp_done[tt])
                ev_tile_t[tt] = mtr_last

            mproj_hist = {}
            for tt in range(NT):
                par = tt % 2
                hb = tt % 4
                own = tt < NTO
                wv_w = [ps_free[tt - 2]] if tt >= 2 else []
                if tt in pre_dh:
                    dh, dh2 = pre_dh[tt]
                else:
                    dh, dh2 = emit_dh(
                        tt, [mproj_hist[tt - 4]] if tt >= 4 else [])
                ncols = 384 if own else 256
                pb = bank(par)
                for dc in range(8):
                    tn(lambda h, pb=pb, dc=dc, nc_=ncols, hb=hb: h.matmul(
                        pb[:, 0:nc_], HTL[hb][:, dc * 128:dc * 128 + 128],
                        A[:, WSLAB_off + dc * 384: WSLAB_off + dc * 384 + nc_],
                        start=(dc == 0), stop=(dc == 7)),
                       waits=([dh] + dwt + wv_w) if dc == 0 else
                       ([dh2] if dc == 4 else []))
                mproj = ("tensor", s.tot["tensor"])
                mproj_hist[tt] = mproj
                off0 = par * 512
                ps_t = P2[0]
                kcos = ap(ps_t, off0, [[1024, 128], [64, 2], [2, 32], [1, 2]])
                kswp = ap(ps_t, off0 + 1, [[1024, 128], [64, 2], [2, 32], [-1, 2]])
                ecv = ap(ECOS, tt * 32, [[1024, 128], [0, 2], [1, 32], [0, 2]])
                esv = ap(ESIN, tt * 64, [[2048, 128], [0, 2], [2, 32], [1, 2]])
                t1v = ap(A, 19584, [[W, 128], [64, 2], [2, 32], [1, 2]])
                t2v = ap(A, 19712, [[W, 128], [64, 2], [2, 32], [1, 2]])
                v(lambda h, kcos=kcos, ecv=ecv, t1v=t1v: h.tensor_tensor(
                    t1v, kcos, ecv, ALU.mult), waits=[mproj])
                v(lambda h, kswp=kswp, esv=esv, t2v=t2v: h.tensor_tensor(
                    t2v, kswp, esv, ALU.mult))
                rope_done[tt] = v(lambda h, par=par: h.tensor_tensor(
                    KR_[par][:, :], T1_[:, :], T2_[:, :], ALU.add),
                    waits=[ev_tile_t[tt - 2]] if tt >= 2 else [])
                for hh2 in range(2):
                    v(lambda h, off0=off0, tt=tt, hh2=hh2: h.tensor_copy(
                        VPH[:, tt * 132 + 66 * hh2: tt * 132 + 66 * hh2 + 64],
                        ps_t[:, off0 + 128 + 64 * hh2: off0 + 192 + 64 * hh2]))
                    mvcp = v(lambda h, off0=off0, tt=tt, hh2=hh2: h.tensor_tensor(
                        VPL[:, tt * 132 + 66 * hh2: tt * 132 + 66 * hh2 + 64],
                        ps_t[:, off0 + 128 + 64 * hh2: off0 + 192 + 64 * hh2],
                        VPH[:, tt * 132 + 66 * hh2: tt * 132 + 66 * hh2 + 64],
                        ALU.subtract))
                vcp_done[tt] = mvcp
                if own:
                    qcos = ap(ps_t, off0 + 256, [[1024, 128], [64, 2], [2, 32], [1, 2]])
                    qswp = ap(ps_t, off0 + 257, [[1024, 128], [64, 2], [2, 32], [-1, 2]])
                    v(lambda h, qcos=qcos, ecv=ecv, t1v=t1v: h.tensor_tensor(
                        t1v, qcos, ecv, ALU.mult))
                    v(lambda h, qswp=qswp, esv=esv, t2v=t2v: h.tensor_tensor(
                        t2v, qswp, esv, ALU.mult))
                    rope_doneq[tt] = v(lambda h, par=par: h.tensor_tensor(
                        QR_[par][:, :], T1_[:, :], T2_[:, :], ALU.add))
                ps_free[tt] = rope_doneq[tt] if own else vcp_done[tt]
                if tt >= 1:
                    emit_tr(tt - 1)
            emit_tr(NT - 1)
            m_projv = ("vector", s.tot["vector"])
            if debug and p_ == 0:
                s.dma("gpsimd", dbg_kp[:, :], A[:, 5120:9216], waits=[m_projv])
                s.dma("gpsimd", dbg_qp[:, :], A[:, 9216:11264], waits=[m_projv])
                s.dma("gpsimd", dbg_vp[:, :], A[:, 11264:15488], waits=[m_projv])
            mcs = ("tensor", s.tot["tensor"])
            cs_ev = []
            for hh in range(2):
                cs_ev.append(v(lambda h, hh=hh: h.tensor_copy(
                    A[0:65, CSH_off + hh: CSH_off + hh + 1],
                    bank(4 + hh)[0:65, 0:1]),
                    waits=[mcs]))
            m_cs = ("vector", s.tot["vector"])
            ep_done = {}
            pending_ep = [None]
            for qg in range(4):
                u_done, usplit_done, av_done = {}, {}, {}

                def emit_qk(kc, qg=qg):
                    wts = []
                    for hh in range(2):
                        if kc >= 2:
                            wts.append(u_done[(hh, kc - 2)])
                    if kc == 0:
                        wts.append(m_cs)
                    for pi, (kp, qp) in enumerate(((KPH, QPH), (KPH, QPL), (KPL, QPH))):
                        for hh in range(2):
                            sb_ = bank(2 * (kc % 2) + hh)
                            tn(lambda h, sb_=sb_, kc=kc, hh=hh, qg=qg, kp=kp, qp=qp,
                               pi=pi: h.matmul(
                                sb_, kp[64 * hh:64 * hh + 64, kc * 128:kc * 128 + 128],
                                qp[64 * hh:64 * hh + 64, qg * 512:qg * 512 + 512],
                                start=(pi == 0), stop=(pi == 2)),
                               waits=wts if (pi == 0 and hh == 0) else [])
                    qk = ("tensor", s.tot["tensor"])
                    for hh in range(2):
                        ui = (kc % 4) * 2 + hh
                        uw = [qk]
                        if kc >= 4:
                            uw.append(av_done[(1, kc - 4)])
                        u_done[(hh, kc)] = sc(lambda h, kc=kc, hh=hh, ui=ui: h.activation(
                            U32[ui], bank(2 * (kc % 2) + hh), AF.Square,
                            bias=cpc("BETA"), scale=cpc("ALPHA")), waits=uw)
                        uhw = [u_done[(hh, kc)]] + ([av_done[(1, kc - 4)]] if kc >= 4 else [])
                        if kc % 2 == 1:
                            sc(lambda h, ui=ui: h.activation(
                                UH[ui][:, :], U32[ui], AF.Identity), waits=uhw)
                            muh = ("scalar", s.tot["scalar"])
                            usplit_done[(hh, kc)] = v(
                                lambda h, ui=ui: h.tensor_tensor(
                                    UL[ui][:, :], U32[ui], UH[ui][:, :], ALU.subtract),
                                waits=[muh])
                            continue
                        v(lambda h, ui=ui: h.tensor_copy(UH[ui][:, :], U32[ui]),
                          waits=uhw)
                        usplit_done[(hh, kc)] = v(lambda h, ui=ui: h.tensor_tensor(
                            UL[ui][:, :], U32[ui], UH[ui][:, :], ALU.subtract))

                def emit_av(kc, qg=qg):
                    for pi, (vp, up) in enumerate(((VPH, UH), (VPH, UL), (VPL, UH))):
                        for hh in range(2):
                            avb = bank(4 + 2 * (qg % 2) + hh)
                            ui = (kc % 4) * 2 + hh
                            avw = []
                            if pi == 0:
                                avw = [usplit_done[(hh, kc)]]
                                if kc == 0 and hh == 0:
                                    if qg == 0:
                                        avw += cs_ev
                                    if qg >= 2:
                                        avw.append(ep_done[(0, qg - 2)])
                                        avw.append(ep_done[(1, qg - 2)])
                            tn(lambda h, kc=kc, hh=hh, avb=avb, vp=vp, up=up, ui=ui,
                               pi=pi: h.matmul(
                                avb[0:66, :],
                                vp[:, kc * 132 + 66 * hh: kc * 132 + 66 * hh + 66],
                                up[ui][:, :],
                                start=(kc == 0 and pi == 0),
                                stop=(kc == NT - 1 and pi == 2)),
                               waits=avw)
                    for hh in range(2):
                        av_done[(hh, kc)] = ("tensor", s.tot["tensor"])

                for kc in range(NT):
                    emit_qk(kc)
                    if kc == 6 and pending_ep[0] is not None:
                        pending_ep[0]()
                        pending_ep[0] = None
                    if kc >= 2:
                        emit_av(kc - 2)
                emit_av(NT - 2)
                emit_av(NT - 1)
                m_av = ("tensor", s.tot["tensor"])
                qp_ = qg % 2
                mav2s = []
                for hh in range(2):
                    avb = bank(4 + 2 * qp_ + hh)
                    v(lambda h, avb=avb, hh=hh, qp_=qp_: h.tensor_scalar(
                        AVS2[qp_][hh][0:65, :], avb[0:65, :],
                        A[0:65, CSH_off + hh: CSH_off + hh + 1], None, ALU.add),
                        waits=[m_av])
                    mav2s.append(v(lambda h, hh=hh, qp_=qp_: h.reciprocal(
                        RDEN2[qp_][hh], AVS2[qp_][hh][64:65, :])))

                def make_ep(qg=qg, qp_=qp_, mav2s=tuple(mav2s)):
                  def run():
                    for hh in range(2):
                        avb = bank(4 + 2 * qp_ + hh)
                        tn(lambda h, avb=avb, hh=hh, qp_=qp_: h.matmul(
                            avb[0:64, :], ONESM[64:65, 0:64],
                            RDEN2[qp_][hh], start=True, stop=True),
                           waits=[mav2s[hh]])
                        m_rr = ("tensor", s.tot["tensor"])
                        ep_done[(hh, qg)] = v(lambda h, hh=hh, qg=qg, qp_=qp_,
                                              avb=avb: h.tensor_tensor(
                            A[0:64, WV_off + hh * 2048 + qg * 512:
                              WV_off + hh * 2048 + qg * 512 + 512],
                            AVS2[qp_][hh][0:64, :], avb[0:64, :], ALU.mult),
                            waits=[m_rr])
                  return run
                pending_ep[0] = make_ep()
            pending_ep[0]()
            pending_ep[0] = None
            m_wv = ("vector", s.tot["vector"])
            wv_spill[p_] = s.dma(
                "gpsimd",
                ap(wv3_d, p_ * 128 * 2048, [[2048, 64], [131072, 2], [1, 2048]]),
                ap(A, WV_off, [[W, 64], [2048, 2], [1, 2048]]),
                waits=[m_wv])
            if p_ < NP - 1:
                preload["dwt"] = emit_wslab(p_ + 1, [mproj_hist[NT - 1]])
                preload["dh"] = {}
                for tt in range(4):
                    wl = [mproj_hist[NT - 4 + tt]]
                    if tt >= 2:
                        wl.append(m_wv)
                    preload["dh"][tt] = emit_dh(tt, wl)
            else:
                preload["dwt"], preload["dh"] = None, {}
            s.barrier()

        WOS = A[:, 0:8192]
        WVB = [A[:, 8192:9216], A[:, 9216:10240]]
        XNB = [A[:, 10240:11264], A[:, 11264:12288]]
        XB4 = [A[:, 12288:13312], A[:, 13312:14336]]
        OT = [A[:, 14336:15360], A[:, 15360:16384]]
        OTMP = A[:, 16384:16896]
        dwo = s.dma("sync", WOS[:, :],
                    ap(wo_d, 0, [[1024, 128], [131072, 8], [1, 1024]]))
        t4_end = {}
        for tto in range(NTO):
            par = tto % 2
            wprev = t4_end.get(tto - 2, [])
            dtoks = []
            for pp in range(8):
                dtoks.append(s.dma(
                    "sync", ap(A, 8192 + par * 1024 + pp * 128, [[W, 128], [1, 128]]),
                    ap(wv3_d, pp * 128 * 2048 + tto * 128, [[2048, 128], [1, 128]]),
                    waits=wprev))
            dtoks.append(s.dma(
                "sync", XNB[par],
                ap(xt3_d, tto * D * 128, [[128, 128], [16384, 8], [1, 128]]),
                waits=wprev))
            dtoks.append(s.dma("sync", XB4[par], x_d[tto * 128:(tto + 1) * 128, :],
                               waits=wprev))
            for nt in range(2):
                pb = bank((tto * 2 + nt) % 4)
                for dc in range(8):
                    tn(lambda h, pb=pb, dc=dc, par=par, nt=nt: h.matmul(
                        pb, WVB[par][:, dc * 128:dc * 128 + 128],
                        WOS[:, dc * 1024 + nt * 512: dc * 1024 + nt * 512 + 512],
                        start=(dc == 0), stop=False),
                       waits=(wprev + dtoks + [dwo]) if dc == 0 else [])
                for dc in range(8):
                    tn(lambda h, pb=pb, dc=dc, par=par, nt=nt: h.matmul(
                        pb, XNB[par][:, dc * 128:dc * 128 + 128],
                        MBUF[:, dc * 1024 + nt * 512: dc * 1024 + nt * 512 + 512],
                        start=False, stop=(dc == 7)))
                mo = ("tensor", s.tot["tensor"])
                v(lambda h, pb=pb, par=par, nt=nt: h.tensor_tensor(
                    OTMP[:, :], pb, XB4[par][:, nt * 512:nt * 512 + 512], ALU.add),
                  waits=[mo, dtoks[-1]] + wprev)
                v(lambda h, par=par, nt=nt: h.tensor_tensor(
                    OT[par][:, nt * 512:nt * 512 + 512], OTMP[:, :],
                    BIAR[:, nt * 512:nt * 512 + 512], ALU.add))
            mot = ("vector", s.tot["vector"])
            s.dma("gpsimd", out_d[tto * 128:(tto + 1) * 128, :], OT[par],
                  waits=[mot] + wprev)
            t4_end[tto] = s.snapshot()

        s.add("vector", lambda h: h.memset(ONESM[0:1, 0:1], 1.0),
              waits=s.snapshot())

        def make_body(ops):
            def body(h):
                last_wait = {}
                for fn, waits, semkey, inc in ops:
                    for (tk, tv) in waits:
                        if last_wait.get(tk, -1) < tv:
                            h.wait_ge(sems[tk], tv)
                            last_wait[tk] = tv
                    inst = fn(h)
                    inst.then_inc(sems[semkey], inc)
            return body

        for en in ENGS:
            if s.ops[en]:
                getattr(blk, en)(make_body(s.ops[en]))

    return nc


def _host_const_arrays():
    cpool = _host_cpool()
    pvals = np.arange(128, dtype=np.float32).reshape(128, 1)
    L = np.arange(32, dtype=np.float64) * (math.log(21.0) / 31.0)
    lhi = np.tile(L.astype(np.float32), (128, 1))
    llo = np.tile((L - L.astype(np.float32).astype(np.float64)).astype(np.float32),
                  (128, 1))
    ident = np.eye(128, dtype=np.float32)
    m = np.outer(np.arange(1024, dtype=np.int64), np.arange(1024, dtype=np.int64)) % 1024
    ang = 2.0 * np.pi * m.astype(np.float64) / 1024.0
    ctab = _wt_layout(np.cos(ang).astype(np.float32), 512)
    stab = _wt_layout(np.sin(ang).astype(np.float32), 512)
    return cpool, pvals, lhi, llo, ident, ctab, stab


def _wt_layout(w, jw):
    r = w.reshape(8, 128, 1024 // jw, jw)
    return np.ascontiguousarray(r.transpose(2, 1, 0, 3)).reshape(w.shape)


_PROG_CACHE = {}

def _get_prog(debug=False):
    if debug not in _PROG_CACHE:
        _PROG_CACHE[debug] = build_program(debug)
    return _PROG_CACHE[debug]


def kernel(L0_attn_x, L0_attn_ln_w, L0_attn_ln_b, L0_attn_w_q, L0_attn_w_kv,
           L0_attn_w_o, L0_attn_theta, L0_ffn_w_freq_real, L0_ffn_w_freq_imag,
           L0_ffn_bias, _debug=False, _cores=8):
    x = np.ascontiguousarray(np.asarray(L0_attn_x, dtype=np.float32))
    cpool, pvals, lhi, llo, ident, ctab, stab = _host_const_arrays()
    common = dict(
        wq=np.ascontiguousarray(np.asarray(L0_attn_w_q, np.float32)),
        wkv=np.ascontiguousarray(np.asarray(L0_attn_w_kv, np.float32)),
        wo=np.ascontiguousarray(np.asarray(L0_attn_w_o, np.float32)),
        lnw=np.asarray(L0_attn_ln_w, np.float32).reshape(1, -1),
        lnb=np.asarray(L0_attn_ln_b, np.float32).reshape(1, -1),
        fbias=np.asarray(L0_ffn_bias, np.float32).reshape(1, -1),
        wfr=_wt_layout(np.asarray(L0_ffn_w_freq_real, np.float32), 128),
        wfi=_wt_layout(np.asarray(L0_ffn_w_freq_imag, np.float32), 128),
        ctab=ctab, stab=stab,
        theta=np.tile(np.asarray(L0_attn_theta, np.float32).reshape(1, 1), (128, 1)),
        cpool=cpool, pvals=pvals, lhi=lhi, llo=llo, ident=ident,
    )
    in_maps = []
    for c in range(_cores):
        b, j = c // 2, c % 2
        xr = np.roll(x[b], -2048 * j, axis=0)
        tvals = (128.0 * ((np.arange(32) + 16 * j) % 32)).astype(np.float32).reshape(32, 1)
        im = dict(common)
        im["x"] = np.ascontiguousarray(xr)
        im["t128vals"] = tvals
        in_maps.append(im)
    nc = _get_prog(_debug)
    res = run_bass_kernel_spmd(nc, in_maps, core_ids=list(range(_cores)))
    out = np.empty((4, 4096, 1024), np.float32)
    for c in range(_cores):
        b, j = c // 2, c % 2
        out[b, 2048 * j:2048 * (j + 1), :] = res.results[c]["out"]
    if _debug:
        return out, res
    return out
